# version 09
# baseline (speedup 1.0000x reference)
import contextlib
import os

os.environ.setdefault("TRN_ALLOW_ICI_ALL_TO_ALL", "1")

import numpy as np
import ml_dtypes

import concourse.bass as bass
import concourse.mybir as mybir
from concourse.bass_utils import run_bass_kernel_spmd

F32 = mybir.dt.float32
BF16 = mybir.dt.bfloat16
FP16 = mybir.dt.float16
AF = mybir.ActivationFunctionType
ALU = mybir.AluOpType
AX = mybir.AxisListType.X

BF = ml_dtypes.bfloat16

T = 4096
C = 2048
D = 128
NHMY = 4
TOK = 1024
I_FF = 4096
SCALE = float(np.float32(1.0 / np.sqrt(128.0)))
C_MAGIC = 12582912.0
K54 = 18014398509481984.0
EPS = 1e-5


def ccum(t8):
    return 2 * t8 * (t8 + 1)


CPH = ccum(8)
TA_LN = [1, 3, 5, 9, 13, 19, 25, 32]
TT_LN = [max(t - 1, 0) for t in TA_LN]


def build_fused():
    nc = bass.Bass("TRN2", target_bir_lowering=False, debug=False)

    x = nc.dram_tensor("x_both", [2 * T, C], F32, kind="ExternalInput").ap()
    wq = nc.dram_tensor("wq", [NHMY, C, D], F32, kind="ExternalInput").ap()
    wk = nc.dram_tensor("wk", [NHMY, C, D], F32, kind="ExternalInput").ap()
    wv = nc.dram_tensor("wv", [NHMY, C, D], F32, kind="ExternalInput").ap()
    bcols = nc.dram_tensor("b_cols", [128, 12], F32, kind="ExternalInput").ap()
    masks_d = nc.dram_tensor("masks", [128, 4 * 512], BF16, kind="ExternalInput").ap()
    ident_d = nc.dram_tensor("ident", [128, 128], BF16, kind="ExternalInput").ap()
    x_own = nc.dram_tensor("x_own", [TOK, C], F32, kind="ExternalInput").ap()
    Wo_d = nc.dram_tensor("Wo", [C, C], F32, kind="ExternalInput").ap()
    bo_d = nc.dram_tensor("bo", [1, C], F32, kind="ExternalInput").ap()
    wup_d = nc.dram_tensor("w_up", [I_FF, C], F32, kind="ExternalInput").ap()
    wdn_d = nc.dram_tensor("w_down", [C, I_FF], F32, kind="ExternalInput").ap()

    out_d = nc.dram_tensor("out", [TOK, C], F32, kind="ExternalOutput").ap()

    cc_y_in = nc.dram_tensor("cc_y_in", [2048, TOK], BF16, kind="Internal").ap()
    cc_l_in = nc.dram_tensor("cc_l_in", [16, TOK], F32, kind="Internal").ap()
    cc_y_out = nc.dram_tensor("cc_y_out", [2048, TOK], BF16, kind="Internal").ap()
    cc_l_out = nc.dram_tensor("cc_l_out", [16, TOK], F32, kind="Internal").ap()
    rl_dram = nc.dram_tensor("rl_scratch", [1, 16 * TOK], F32, kind="Internal").ap()
    x2_dram = nc.dram_tensor("x2_scratch", [128, 8 * C], F32, kind="Internal").ap()

    NT1 = 32

    oes = contextlib.ExitStack()
    with oes:
        ident = oes.enter_context(nc.sbuf_tensor("ident_sb", [128, 128], BF16))
        absacc = oes.enter_context(nc.sbuf_tensor("absacc", [128, 64], F32))
        stat1 = oes.enter_context(nc.sbuf_tensor("stat1", [128, 1], F32))
        stat2 = oes.enter_context(nc.sbuf_tensor("stat2", [128, 1], F32))
        ccs = oes.enter_context(nc.semaphore("ccs"))
        ld_ws = oes.enter_context(nc.semaphore("ld_ws"))
        ld_wsg = oes.enter_context(nc.semaphore("ld_wsg"))
        v_abs = oes.enter_context(nc.semaphore("v_abs"))
        es = contextlib.ExitStack()
        LNx = es.enter_context(nc.sbuf_tensor("LNx", [128, 16 * T], BF16))
        masks = es.enter_context(nc.sbuf_tensor("masks_sb", [128, 4 * 512], BF16))
        bcol = es.enter_context(nc.sbuf_tensor("bcol", [128, 12], F32))
        ones_bf = es.enter_context(nc.sbuf_tensor("ones_bfA", [128, 1], BF16))
        wq_sb = es.enter_context(nc.sbuf_tensor("wq_sb", [128, 2048], BF16))
        wk_sb = es.enter_context(nc.sbuf_tensor("wk_sb", [128, 2048], BF16))
        wv_sb = es.enter_context(nc.sbuf_tensor("wv_sb", [128, 2048], BF16))
        K_fm = es.enter_context(nc.sbuf_tensor("K_fm", [128, T], BF16))
        Q_fm = es.enter_context(nc.sbuf_tensor("Q_fm", [128, T], BF16))
        V_tm = es.enter_context(nc.sbuf_tensor("V_tm", [128, T], BF16))
        y_sb = es.enter_context(nc.sbuf_tensor("y_sb", [128, T], BF16))
        P_sb = es.enter_context(nc.sbuf_tensor("P_sb", [128, 4 * 512], BF16))
        l_tile = es.enter_context(nc.sbuf_tensor("l_tile", [1, 2 * 512], F32))
        xt = es.enter_context(nc.sbuf_tensor("xt", [128, 2 * C], F32, side="right"))
        ln_tm = es.enter_context(nc.sbuf_tensor("ln_tm", [128, C], BF16))
        scrapf = es.enter_context(nc.sbuf_tensor("scrapf", [128, 512], F32, side="right"))
        ssum = es.enter_context(nc.sbuf_tensor("ssum", [128, 2], F32))
        ssq = es.enter_context(nc.sbuf_tensor("ssq", [128, 8], F32))
        m_c = es.enter_context(nc.sbuf_tensor("m_c", [128, 1], F32))
        nmsq = es.enter_context(nc.sbuf_tensor("nmsq", [128, 1], F32))
        v_c = es.enter_context(nc.sbuf_tensor("v_c", [128, 1], F32))
        u_c = es.enter_context(nc.sbuf_tensor("u_c", [128, 2], F32))
        msq = es.enter_context(nc.sbuf_tensor("msq", [128, 1], F32))
        sn1 = es.enter_context(nc.sbuf_tensor("sn1", [128, 1], F32))
        sn2 = es.enter_context(nc.sbuf_tensor("sn2", [128, 1], F32))
        sn3 = es.enter_context(nc.sbuf_tensor("sn3", [128, 1], F32))
        epsc = es.enter_context(nc.sbuf_tensor("epsc", [128, 1], F32))
        c15 = es.enter_context(nc.sbuf_tensor("c15", [128, 1], F32))
        rv = es.enter_context(nc.sbuf_tensor("rv", [128, 2], F32))
        r0c = es.enter_context(nc.sbuf_tensor("r0c", [128, 1], F32))
        nta = es.enter_context(nc.sbuf_tensor("nta", [128, 1], F32))
        r_col = es.enter_context(nc.sbuf_tensor("r_col", [128, 1], F32))
        negmr = es.enter_context(nc.sbuf_tensor("negmr", [128, 1], F32))
        ps_tr = es.enter_context(nc.psum_tensor("ps_trA", [128, 2048], F32))
        pstb = es.enter_context(nc.psum_tensor("pstb", [128, 2048], BF16))
        psY = es.enter_context(nc.psum_tensor("psY", [128, 512], F32))
        psL = es.enter_context(nc.psum_tensor("psL", [1, 512], F32))
        ld_c = oes.enter_context(nc.semaphore("ld_c"))
        ld_xe = oes.enter_context(nc.semaphore("ld_xA0"))
        ld_xod = oes.enter_context(nc.semaphore("ld_xA1"))
        ld_w = oes.enter_context(nc.semaphore("ld_w"))
        v_stat = oes.enter_context(nc.semaphore("v_stat"))
        s_r = oes.enter_context(nc.semaphore("s_r"))
        v_nm = oes.enter_context(nc.semaphore("v_nm"))
        s_ln = oes.enter_context(nc.semaphore("s_ln"))
        t_tr = oes.enter_context(nc.semaphore("t_tr"))
        v_ev = oes.enter_context(nc.semaphore("v_ev"))
        t_qkv = oes.enter_context(nc.semaphore("t_qkv"))
        s_ev = oes.enter_context(nc.semaphore("s_ev"))
        t_vtr = oes.enter_context(nc.semaphore("t_vtr"))
        v_vt = oes.enter_context(nc.semaphore("v_vt"))
        t_s = oes.enter_context(nc.semaphore("t_s"))
        s_exp = oes.enter_context(nc.semaphore("s_exp"))
        v_mk = oes.enter_context(nc.semaphore("v_mk"))
        t_pv = oes.enter_context(nc.semaphore("t_pv"))
        v_fin = oes.enter_context(nc.semaphore("v_fin"))
        q_l = oes.enter_context(nc.semaphore("q_l"))
        st_y = oes.enter_context(nc.semaphore("st_y"))
        v_on = oes.enter_context(nc.semaphore("v_on"))
        s_u = oes.enter_context(nc.semaphore("s_u"))
        v_rv = oes.enter_context(nc.semaphore("v_rv"))
        block = es.enter_context(nc.Block())

        @block.sync
        def _(sync):
            sync.dma_start(out=masks[:], in_=masks_d).then_inc(ld_c, 16)
            sync.dma_start(out=ident[:], in_=ident_d).then_inc(ld_c, 16)
            sync.dma_start(out=bcol[:], in_=bcols).then_inc(ld_c, 16)
            for tg in range(2 * NT1):
                if tg >= 2:
                    sync.wait_ge(v_stat, tg - 1)
                    sync.wait_ge(s_ln, tg - 1)
                sync.dma_start(
                    out=xt[:, (tg % 2) * C:(tg % 2 + 1) * C],
                    in_=x[tg * 128:(tg + 1) * 128, :],
                ).then_inc(ld_xe if tg % 2 == 0 else ld_xod, 16)
            sync.wait_ge(s_ln, 64)
            for j in range(0, 64, 2):
                if j >= 2:
                    sync.wait_ge(v_abs, j - 1)
                if j < 32:
                    src_w = wup_d[j * 128:(j + 1) * 128, :]
                else:
                    jj = j - 32
                    src_w = wdn_d[(jj // 2) * 128:(jj // 2 + 1) * 128,
                                  (jj % 2) * 2048:(jj % 2 + 1) * 2048]
                sync.dma_start(out=xt[:, 0:C],
                               in_=src_w).then_inc(ld_ws, 16)

        @block.gpsimd
        def _(gpsimd):
            for h in range(NHMY):
                if h > 0:
                    gpsimd.wait_ge(t_qkv, 24 * h)
                for (dst, srcw) in ((wq_sb, wq), (wk_sb, wk), (wv_sb, wv)):
                    gpsimd.dma_start(
                        out=dst.ap().rearrange("p (ch d) -> p ch d", ch=16),
                        in_=srcw[h].rearrange("(ch p) d -> p ch d", p=128),
                    ).then_inc(ld_w, 16)
            gpsimd.wait_ge(s_ln, 64)
            for j in range(1, 64, 2):
                if j >= 2:
                    gpsimd.wait_ge(v_abs, j - 1)
                if j < 32:
                    src_w = wup_d[j * 128:(j + 1) * 128, :]
                else:
                    jj = j - 32
                    src_w = wdn_d[(jj // 2) * 128:(jj // 2 + 1) * 128,
                                  (jj % 2) * 2048:(jj % 2 + 1) * 2048]
                gpsimd.dma_start(out=xt[:, C:2 * C],
                                 in_=src_w).then_inc(ld_wsg, 16)

        def ln_v_unit(vector, tl, off, emit_evac=True):
            tg = off + tl
            c = tg % 2
            if tl >= 1:
                ln_v_recip(vector, tg - 1)
            if emit_evac and tl >= 2:
                ln_v_evac(vector, tg - 2, off)
            vector.wait_ge(ld_xe if tg % 2 == 0 else ld_xod,
                           16 * (tg // 2 + 1))
            nc.vector.reduce_sum(ssum[:, c:c + 1], xt[:, c * C:(c + 1) * C],
                                 axis=AX)
            for qq in range(4):
                sl = slice(c * C + qq * 512, c * C + (qq + 1) * 512)
                mm = nc.vector.scalar_tensor_tensor(scrapf[:], xt[:, sl],
                                                    1.0, xt[:, sl],
                                                    op0=ALU.mult, op1=ALU.mult,
                                                    accum_out=ssq[:, 4 * c + qq:4 * c + qq + 1])
            mm.then_inc(v_stat, 1)

        def ln_v_recip(vector, tg):
            c = tg % 2
            vector.wait_ge(s_u, tg + 1)
            nc.vector.reciprocal(rv[:, c:c + 1], u_c[:, c:c + 1]
                                 ).then_inc(v_rv, 1)

        def ln_v_evac(vector, tg, off):
            tl = tg - off
            vector.wait_ge(t_tr, tg + 1)
            if tg == NT1:
                vector.wait_ge(t_qkv, 48)
            nc.vector.tensor_copy(
                LNx.ap().rearrange("p (ch t) -> p ch t", ch=16)[:, :, tl * 128:(tl + 1) * 128],
                pstb.ap().rearrange("p (ch t) -> p ch t", ch=16),
            ).then_inc(v_ev, 1)

        def ln_s_unit(scalar, tl, off):
            tg = off + tl
            if tl >= 1:
                ln_s_part2(scalar, tg - 1)
            ln_s_part1(scalar, tg)

        def ln_s_part1(scalar, tg):
            c = tg % 2
            scalar.wait_ge(v_stat, tg + 1)
            nc.scalar.activation(m_c[:], ssum[:, c:c + 1], AF.Identity,
                                 scale=1.0 / C)
            nc.scalar.activation(msq[:], m_c[:], AF.Square)
            nc.scalar.activation(nmsq[:], msq[:], AF.Identity, scale=-1.0,
                                 bias=epsc[:, 0:1])
            nc.scalar.activation(sn1[:], ssq[:, 4 * c:4 * c + 1], AF.Identity,
                                 bias=ssq[:, 4 * c + 1:4 * c + 2])
            nc.scalar.activation(sn2[:], sn1[:], AF.Identity,
                                 bias=ssq[:, 4 * c + 2:4 * c + 3])
            nc.scalar.activation(sn3[:], sn2[:], AF.Identity,
                                 bias=ssq[:, 4 * c + 3:4 * c + 4])
            nc.scalar.drain()
            nc.scalar.activation(u_c[:, c:c + 1], sn3[:], AF.Identity,
                                 scale=1.0 / C, bias=nmsq[:, 0:1]
                                 ).then_inc(s_u, 1)

        def ln_s_part2(scalar, tg):
            c = tg % 2
            scalar.wait_ge(v_rv, tg + 1)
            nc.scalar.activation(r0c[:], rv[:, c:c + 1], AF.Sqrt)
            nc.scalar.activation(sn1[:], r0c[:], AF.Square)
            nc.scalar.drain()
            nc.scalar.activation(sn2[:], sn1[:], AF.Identity,
                                 scale=u_c[:, c:c + 1])
            nc.scalar.activation(sn3[:], sn2[:], AF.Identity, scale=-0.5,
                                 bias=c15[:, 0:1])
            nc.scalar.drain()
            nc.scalar.activation(r_col[:], sn3[:], AF.Identity,
                                 scale=r0c[:, 0:1])
            nc.scalar.activation(sn1[:], m_c[:], AF.Identity, scale=-1.0)
            nc.scalar.drain()
            nc.scalar.activation(negmr[:], sn1[:], AF.Identity,
                                 scale=r_col[:, 0:1])
            if tg >= 1:
                scalar.wait_ge(t_tr, tg)
            nc.scalar.drain()
            nc.scalar.activation(ln_tm[:], xt[:, c * C:(c + 1) * C], AF.Identity,
                                 bias=negmr[:, 0:1], scale=r_col[:, 0:1]
                                 ).then_inc(s_ln, 1)

        def wst_v_unit(vector, j):
            vector.wait_ge(ld_ws if j % 2 == 0 else ld_wsg,
                           16 * (j // 2 + 1))
            nc.vector.reduce_sum(absacc[:, j:j + 1],
                                 xt[:, (j % 2) * C:(j % 2 + 1) * C],
                                 axis=AX, apply_absolute_value=True
                                 ).then_inc(v_abs, 1)

        def ph1_vector(vector, p):
            off = NT1 * p
            for tl in range(NT1):
                ln_v_unit(vector, tl, off)
            ln_v_recip(vector, off + NT1 - 1)
            ln_v_evac(vector, off + NT1 - 2, off)
            ln_v_evac(vector, off + NT1 - 1, off)

        def ph1_scalar(scalar, p):
            off = NT1 * p
            for tl in range(NT1):
                ln_s_unit(scalar, tl, off)
            ln_s_part2(scalar, off + NT1 - 1)

        def ln_t_unit(tensor, tg):
            tensor.wait_ge(s_ln, tg + 1)
            if tg > 0:
                tensor.wait_ge(v_ev, tg)
            mm = None
            for i in range(16):
                mm = nc.tensor.transpose(pstb[:, i * 128:(i + 1) * 128],
                                         ln_tm[:, i * 128:(i + 1) * 128],
                                         ident[:])
            mm.then_inc(t_tr, 1)

        def ph1_tensor(tensor, p):
            if p == 1:
                tensor.wait_ge(v_vt, 16)
            for tl in range(NT1):
                ln_t_unit(tensor, NT1 * p + tl)

        def ph2_vector(vector, h):
            for g in range(8):
                vector.wait_ge(t_vtr, 32 * h + 4 * (g + 1))
                nc.vector.tensor_copy(
                    V_tm[:, g * 512:(g + 1) * 512],
                    pstb[:, (g % 2) * 1024:(g % 2) * 1024 + 512],
                ).then_inc(v_vt, 1)
            for t8 in range(8):
                gbase = CPH * h + ccum(t8)
                nkc = 4 * (t8 + 1)
                for kcl in range(4):
                    g = gbase + 4 * t8 + kcl
                    slot = g % 4
                    vector.wait_ge(s_exp, g + 1)
                    nc.vector.tensor_tensor(
                        P_sb[:, slot * 512:(slot + 1) * 512],
                        P_sb[:, slot * 512:(slot + 1) * 512],
                        masks[:, kcl * 512:(kcl + 1) * 512],
                        ALU.mult).then_inc(v_mk, 1)
                ti = 8 * h + t8
                vector.wait_ge(t_pv, gbase + nkc)
                if t8 == 0 and h > 0:
                    vector.wait_ge(st_y, 64 * h)
                nc.vector.tensor_copy(y_sb[:, t8 * 512:(t8 + 1) * 512], psY[:])
                if ti >= 2:
                    vector.wait_ge(q_l, 16 * (ti - 1))
                nc.vector.tensor_copy(
                    l_tile[0:1, (ti % 2) * 512:(ti % 2 + 1) * 512],
                    psL[:]).then_inc(v_fin, 1)
                if h == 1:
                    for tl in range(TA_LN[t8 - 1] if t8 else 0, TA_LN[t8]):
                        ln_v_unit(vector, tl, NT1)
                    if t8 == 7:
                        ln_v_recip(vector, 2 * NT1 - 1)
                        ln_v_evac(vector, 2 * NT1 - 2, NT1)
                        ln_v_evac(vector, 2 * NT1 - 1, NT1)
                if h >= 2:
                    for j in range(4 * t8, 4 * t8 + 4):
                        wst_v_unit(vector, 32 * (h - 2) + j)
                    if h == 3 and t8 == 7:
                        nc.vector.reduce_sum(stat1[:], absacc[:, 0:32], axis=AX)
                        nc.vector.reduce_sum(stat2[:], absacc[:, 32:64], axis=AX)
                        vector.drain()
                        vector.nop().then_inc(v_abs, 1)

        def ph2_scalar(scalar, h):
            b, hl = h // 2, h % 2
            if h > 0:
                scalar.wait_ge(st_y, 64 * h)
            for gi, (dst, bc) in enumerate(((K_fm, 4), (Q_fm, 0), (y_sb, 8))):
                for t8 in range(8):
                    grp = 24 * h + gi * 8 + t8
                    scalar.wait_ge(t_qkv, grp + 1)
                    nc.scalar.activation(
                        dst[:, t8 * 512:(t8 + 1) * 512],
                        ps_tr[:, (grp % 2) * 512:((grp % 2) + 1) * 512],
                        AF.Identity, bias=bcol[:, bc + h:bc + h + 1],
                        scale=1.0).then_inc(s_ev, 1)
            for t8 in range(8):
                gbase = CPH * h + ccum(t8)
                nkc = 4 * (t8 + 1)
                for kc in range(nkc):
                    g = gbase + kc
                    slot = g % 4
                    scalar.wait_ge(t_s, g + 1)
                    if g >= 4:
                        scalar.wait_ge(t_pv, g - 3)
                    nc.scalar.activation(
                        P_sb[:, slot * 512:(slot + 1) * 512],
                        ps_tr[:, 1024 + (g % 2) * 512:1024 + ((g % 2) + 1) * 512],
                        AF.Exp, scale=SCALE).then_inc(s_exp, 1)
                ti = 8 * h + t8
                scalar.wait_ge(v_fin, ti + 1)
                scalar.dma_start(
                    out=cc_l_in[(4 * b + t8 // 2) * 2 + hl:
                                (4 * b + t8 // 2) * 2 + hl + 1,
                                (t8 % 2) * 512:(t8 % 2) * 512 + 512],
                    in_=l_tile[0:1, (ti % 2) * 512:(ti % 2 + 1) * 512],
                ).then_inc(q_l, 16)
                if h == 1:
                    for tl in range(TA_LN[t8 - 1] if t8 else 0, TA_LN[t8]):
                        ln_s_unit(scalar, tl, NT1)
                    if t8 == 7:
                        ln_s_part2(scalar, 2 * NT1 - 1)
            scalar.wait_ge(v_fin, 8 * (h + 1))
            for qq in range(4):
                scalar.dma_start(
                    out=cc_y_in[(4 * b + qq) * 256 + hl * 128:
                                (4 * b + qq) * 256 + (hl + 1) * 128, :],
                    in_=y_sb[:, qq * TOK:(qq + 1) * TOK]).then_inc(st_y, 16)

        def ph2_tensor(tensor, h):
            tensor.wait_ge(ld_w, 48 * (h + 1))
            if h == 0:
                tensor.wait_ge(v_ev, NT1)
                tensor.wait_ge(v_on, 1)
            else:
                tensor.wait_ge(v_vt, 8 * h)
            if h == 2:
                tensor.wait_ge(v_ev, 2 * NT1)
            for gi, wsb in enumerate((wk_sb, wq_sb, wv_sb)):
                for t8 in range(8):
                    grp = 24 * h + gi * 8 + t8
                    if grp >= 2:
                        tensor.wait_ge(s_ev, grp - 1)
                    mm = None
                    for ch in range(16):
                        mm = nc.tensor.matmul(
                            ps_tr[:, (grp % 2) * 512:((grp % 2) + 1) * 512],
                            wsb[:, ch * 128:(ch + 1) * 128],
                            LNx[:, ch * T + t8 * 512: ch * T + (t8 + 1) * 512],
                            start=(ch == 0), stop=(ch == 15))
                    mm.then_inc(t_qkv, 1)
            tensor.wait_ge(s_ev, 24 * (h + 1))
            tensor.wait_ge(v_fin, 8 * h)
            for g in range(8):
                if 8 * h + g >= 2:
                    tensor.wait_ge(v_vt, 8 * h + g - 1)
                mm = None
                for i in range(4):
                    kc = 4 * g + i
                    mm = nc.tensor.transpose(
                        pstb[:, (g % 2) * 1024 + i * 128:(g % 2) * 1024 + (i + 1) * 128],
                        y_sb[:, kc * 128:(kc + 1) * 128],
                        ident[:])
                mm.then_inc(t_vtr, 4)
            tensor.wait_ge(v_vt, 8 * (h + 1))
            for t8 in range(8):
                gbase = CPH * h + ccum(t8)
                nkc = 4 * (t8 + 1)
                ti = 8 * h + t8
                for kc in range(nkc):
                    g = gbase + kc
                    if g >= 2:
                        tensor.wait_ge(s_exp, g - 1)
                    nc.tensor.matmul(
                        ps_tr[:, 1024 + (g % 2) * 512:1024 + ((g % 2) + 1) * 512],
                        K_fm[:, kc * 128:(kc + 1) * 128],
                        Q_fm[:, t8 * 512:(t8 + 1) * 512],
                        start=True, stop=True).then_inc(t_s, 1)
                    slot = g % 4
                    if kc >= 4 * t8:
                        tensor.wait_ge(v_mk, 32 * h + 4 * t8 + (kc - 4 * t8) + 1)
                    else:
                        tensor.wait_ge(s_exp, g + 1)
                    if kc == 0:
                        tensor.wait_ge(v_fin, ti)
                    nc.tensor.matmul(psL[:], ones_bf[:],
                                     P_sb[:, slot * 512:(slot + 1) * 512],
                                     start=(kc == 0), stop=(kc == nkc - 1))
                    nc.tensor.matmul(psY[:],
                                     V_tm[:, kc * 128:(kc + 1) * 128],
                                     P_sb[:, slot * 512:(slot + 1) * 512],
                                     start=(kc == 0), stop=(kc == nkc - 1)
                                     ).then_inc(t_pv, 1)
                if h == 1:
                    for tl in range(TT_LN[t8 - 1] if t8 else 0, TT_LN[t8]):
                        ln_t_unit(tensor, NT1 + tl)
                    if t8 == 7:
                        ln_t_unit(tensor, 2 * NT1 - 1)

        @block.vector
        def _(vector):
            nc.vector.memset(ones_bf[:], 1.0).then_inc(v_on, 1)
            nc.vector.memset(epsc[:], EPS)
            nc.vector.memset(c15[:], 1.5)
            vector.drain()
            ph1_vector(vector, 0)
            vector.wait_ge(ld_c, 48)
            ph2_vector(vector, 0)
            ph2_vector(vector, 1)
            ph2_vector(vector, 2)
            ph2_vector(vector, 3)

        @block.scalar
        def _(scalar):
            ph1_scalar(scalar, 0)
            scalar.wait_ge(ld_c, 48)
            ph2_scalar(scalar, 0)
            ph2_scalar(scalar, 1)
            ph2_scalar(scalar, 2)
            ph2_scalar(scalar, 3)

        @block.tensor
        def _(tensor):
            tensor.wait_ge(ld_c, 48)
            ph1_tensor(tensor, 0)
            ph2_tensor(tensor, 0)
            ph2_tensor(tensor, 1)
            ph2_tensor(tensor, 2)
            ph2_tensor(tensor, 3)

        es.close()


        ones_bf = oes.enter_context(nc.sbuf_tensor("ones_bf", [1, 128], BF16))
        ones_f1 = oes.enter_context(nc.sbuf_tensor("ones_f1", [1, 128], F32))
        ones_fc = oes.enter_context(nc.sbuf_tensor("ones_fc", [128, 1], F32))
        cmag = oes.enter_context(nc.sbuf_tensor("cmag", [128, 1], F32))
        negc = oes.enter_context(nc.sbuf_tensor("negc", [128, 1], F32))
        prA = oes.enter_context(nc.sbuf_tensor("prA", [128, 1], F32))
        prB = oes.enter_context(nc.sbuf_tensor("prB", [128, 1], F32))
        prF = oes.enter_context(nc.sbuf_tensor("prF", [128, 1], F32))
        bo_bf = oes.enter_context(nc.sbuf_tensor("bo_bf", [1, C], BF16))
        l_sb = oes.enter_context(nc.sbuf_tensor("l_sb", [128, 128], F32))
        rl128 = oes.enter_context(nc.sbuf_tensor("rl128", [128, 128], F32))
        s_cols = oes.enter_context(nc.sbuf_tensor("s_cols", [128, 4], F32))
        sc_small = oes.enter_context(nc.sbuf_tensor("sc_small", [1, 8], F32))
        au_all = oes.enter_context(nc.sbuf_tensor("au_all", [128, 8], F32))
        ad_all = oes.enter_context(nc.sbuf_tensor("ad_all", [128, 8], F32))
        pmax_all = oes.enter_context(nc.sbuf_tensor("pmax_all", [128, 64], F32))
        bn_out = oes.enter_context(nc.sbuf_tensor("bn_out", [128, 24], F32))
        mv = oes.enter_context(nc.sbuf_tensor("mv", [128, 2], F32))
        t_a = oes.enter_context(nc.sbuf_tensor("t_a", [128, 1], F32))
        t_b = oes.enter_context(nc.sbuf_tensor("t_b", [128, 1], F32))
        t_c = oes.enter_context(nc.sbuf_tensor("t_c", [128, 1], F32))
        r5 = oes.enter_context(nc.sbuf_tensor("r5", [128, 1], F32))
        s1c = oes.enter_context(nc.sbuf_tensor("s1c", [128, 1], F32))
        asc = oes.enter_context(nc.sbuf_tensor("asc", [128, 1], F32))
        abi = oes.enter_context(nc.sbuf_tensor("abi", [128, 1], F32))
        s2c = oes.enter_context(nc.sbuf_tensor("s2c", [128, 1], F32))
        psacc = oes.enter_context(nc.psum_tensor("psacc", [128, 2 * 512], F32))
        ps_tr = oes.enter_context(nc.psum_tensor("ps_tr", [128, 2048], BF16))
        ps_tr2 = oes.enter_context(nc.psum_tensor("ps_tr2", [128, 2048], BF16))
        psB = oes.enter_context(nc.psum_tensor("psB", [128, 512], F32))
        ld0 = oes.enter_context(nc.semaphore("ld0"))
        ldg = oes.enter_context(nc.semaphore("ldg"))
        ldw = oes.enter_context(nc.semaphore("ldw"))
        v_pr = oes.enter_context(nc.semaphore("v_pr"))
        t_st = oes.enter_context(nc.semaphore("t_st"))
        v_sfin = oes.enter_context(nc.semaphore("v_sfin"))
        v_rl = oes.enter_context(nc.semaphore("v_rl"))
        q_rl = oes.enter_context(nc.semaphore("q_rl"))
        ld_y = oes.enter_context(nc.semaphore("ld_y"))
        ld_xo = oes.enter_context(nc.semaphore("ld_xo"))
        t_bc = oes.enter_context(nc.semaphore("t_bc"))
        v_bc = oes.enter_context(nc.semaphore("v_bc"))
        v_yn = oes.enter_context(nc.semaphore("v_yn"))
        t_wo = oes.enter_context(nc.semaphore("t_wo"))
        v_x2 = oes.enter_context(nc.semaphore("v_x2"))
        v_x2b = oes.enter_context(nc.semaphore("v_x2b"))
        v_st5 = oes.enter_context(nc.semaphore("v_st5"))
        s_r5 = oes.enter_context(nc.semaphore("s_r5"))
        v_ab5 = oes.enter_context(nc.semaphore("v_ab5"))
        s_z5 = oes.enter_context(nc.semaphore("s_z5"))
        v_rq5 = oes.enter_context(nc.semaphore("v_rq5"))
        t_tr5 = oes.enter_context(nc.semaphore("t_tr5"))
        v_xq = oes.enter_context(nc.semaphore("v_xq"))
        s_q6 = oes.enter_context(nc.semaphore("s_q6"))
        v_q6 = oes.enter_context(nc.semaphore("v_q6"))
        t_tw6 = oes.enter_context(nc.semaphore("t_tw6"))
        v_tw6 = oes.enter_context(nc.semaphore("v_tw6"))
        t_up = oes.enter_context(nc.semaphore("t_up"))
        s_h6 = oes.enter_context(nc.semaphore("s_h6"))
        v_hm = oes.enter_context(nc.semaphore("v_hm"))
        v_st7 = oes.enter_context(nc.semaphore("v_st7"))
        v_rq7 = oes.enter_context(nc.semaphore("v_rq7"))
        t_tr7 = oes.enter_context(nc.semaphore("t_tr7"))
        v_hq = oes.enter_context(nc.semaphore("v_hq"))
        s_q8 = oes.enter_context(nc.semaphore("s_q8"))
        v_q8 = oes.enter_context(nc.semaphore("v_q8"))
        t_tw8 = oes.enter_context(nc.semaphore("t_tw8"))
        v_tw8 = oes.enter_context(nc.semaphore("v_tw8"))
        t_dn = oes.enter_context(nc.semaphore("t_dn"))
        s_o8 = oes.enter_context(nc.semaphore("s_o8"))
        v_o8 = oes.enter_context(nc.semaphore("v_o8"))
        st_out = oes.enter_context(nc.semaphore("st_out"))
        q_x2s = oes.enter_context(nc.semaphore("q_x2s"))
        q_x2r = oes.enter_context(nc.semaphore("q_x2r"))
        ldwg = oes.enter_context(nc.semaphore("ldwg"))
        with (
            nc.Block() as b0,
        ):
            @b0.sync
            def _(sync):
                sync.wait_ge(ccs, 2)
                sync.dma_start(
                    out=l_sb[:],
                    in_=cc_l_out.rearrange("g (r f) -> (g r) f", f=128),
                ).then_inc(ld0, 16)

            @b0.gpsimd
            def _(gpsimd):
                gpsimd.wait_ge(st_y, 16 * 4 * NHMY)
                gpsimd.wait_ge(q_l, 16 * 8 * NHMY)
                nc.gpsimd.collective_compute(
                    "AllToAll", ALU.bypass,
                    ins=[cc_y_in], outs=[cc_y_out],
                    replica_groups=[list(range(8))],
                ).then_inc(ccs, 1)
                nc.gpsimd.collective_compute(
                    "AllToAll", ALU.bypass,
                    ins=[cc_l_in], outs=[cc_l_out],
                    replica_groups=[list(range(8))],
                ).then_inc(ccs, 1)
                gpsimd.dma_start(out=bo_bf[:], in_=bo_d).then_inc(ldg, 16)

            @b0.scalar
            def _(scalar):
                scalar.wait_ge(v_rl, 1)
                scalar.dma_start(
                    out=rl_dram.rearrange("a (p f) -> (a p) f", p=128),
                    in_=rl128[:]).then_inc(q_rl, 16)

            @b0.vector
            def _(vector):
                nc.vector.memset(ones_bf[:], 1.0)
                nc.vector.memset(ones_f1[:], 1.0)
                nc.vector.memset(ones_fc[:], 1.0)
                nc.vector.memset(prA[:], 8388608.0)
                vector.drain()
                nc.vector.tensor_scalar_add(prB[:], prA[:], 0.25)
                vector.drain()
                nc.vector.tensor_sub(prF[:], prB[:], prA[:])
                vector.drain()
                nc.vector.tensor_scalar(cmag[:], prF[:], K54, C_MAGIC,
                                        op0=ALU.mult, op1=ALU.add)
                vector.drain()
                nc.vector.tensor_scalar_mul(negc[:], cmag[:], -1.0)
                vector.drain()
                vector.nop().then_inc(v_pr, 1)
                vector.nop().then_inc(v_pr, 1)
                vector.wait_ge(t_st, 2)
                nc.vector.tensor_scalar(sc_small[0:1, 0:2], psB[0:1, 0:2],
                                        1.0 / 8388608.0, EPS,
                                        op0=ALU.mult, op1=ALU.max)
                vector.drain()
                nc.vector.reciprocal(sc_small[0:1, 2:4], sc_small[0:1, 0:2])
                vector.drain()
                vector.nop().then_inc(v_pr, 1)
                vector.wait_ge(t_st, 3)
                nc.vector.tensor_copy(s_cols[:, 0:4], psB[:, 2:6]).then_inc(v_sfin, 1)
                vector.wait_ge(ld0, 16)
                nc.vector.reciprocal(rl128[:], l_sb[:]).then_inc(v_rl, 1)

            @b0.tensor
            def _(tensor):
                tensor.wait_ge(v_pr, 2)
                tensor.wait_ge(v_abs, 65)
                nc.tensor.matmul(psB[0:1, 0:1], ones_fc[:], stat1[:],
                                 start=True, stop=True)
                nc.tensor.matmul(psB[0:1, 1:2], ones_fc[:], stat2[:],
                                 start=True, stop=True).then_inc(t_st, 2)
                tensor.wait_ge(v_pr, 3)
                nc.tensor.matmul(psB[:, 2:3], ones_f1[:], sc_small[0:1, 2:3],
                                 start=True, stop=True)
                nc.tensor.matmul(psB[:, 3:4], ones_f1[:], sc_small[0:1, 0:1],
                                 start=True, stop=True)
                nc.tensor.matmul(psB[:, 4:5], ones_f1[:], sc_small[0:1, 3:4],
                                 start=True, stop=True)
                nc.tensor.matmul(psB[:, 5:6], ones_f1[:], sc_small[0:1, 1:2],
                                 start=True, stop=True).then_inc(t_st, 1)

        x2_es = contextlib.ExitStack()
        x2 = x2_es.enter_context(nc.sbuf_tensor("x2", [128, 8 * C], F32, side="right"))
        with (
            nc.sbuf_tensor("y_all", [128, 16 * TOK], BF16) as y_all,
            nc.sbuf_tensor("rl_buf", [1, 2 * TOK], F32) as rl_buf,
            nc.sbuf_tensor("rlb", [128, 512], F32) as rlb,
            nc.sbuf_tensor("wo_bf", [128, 2 * 16 * 512], BF16) as wo_bf,
            nc.sbuf_tensor("xo", [128, 2 * C], F32) as xo,
            nc.Block() as b4,
        ):
            @b4.sync
            def _(sync):

                sync.dma_start(
                    out=y_all.ap().rearrange("p (hc t) -> p hc t", hc=16),
                    in_=cc_y_out.rearrange("(hc p) t -> p hc t", p=128),
                ).then_inc(ld_y, 16)
                for tt in range(8):
                    if tt >= 2:
                        sync.wait_ge(v_x2b, tt - 1)
                    sync.dma_start(out=xo[:, (tt % 2) * C:(tt % 2 + 1) * C],
                                   in_=x_own[tt * 128:(tt + 1) * 128, :]
                                   ).then_inc(ld_xo, 16)

            @b4.scalar
            def _(scalar):
                scalar.wait_ge(q_rl, 16)
                for hc in range(16):
                    if hc >= 2:
                        scalar.wait_ge(t_bc, 2 * (hc - 1))
                    scalar.dma_start(
                        out=rl_buf[0:1, (hc % 2) * TOK:(hc % 2 + 1) * TOK],
                        in_=rl_dram[0:1, hc * TOK:(hc + 1) * TOK],
                    ).then_inc(q_rl, 16)

            @b4.gpsimd
            def _(gpsimd):
                for oc in range(4):
                    if oc >= 2:
                        gpsimd.wait_ge(t_wo, 8 * (oc - 1))
                    gpsimd.dma_start(
                        out=wo_bf.ap().rearrange("p (b hc o) -> p b hc o", b=2, hc=16)[:, oc % 2],
                        in_=Wo_d[:, oc * 512:(oc + 1) * 512].rearrange(
                            "(hc p) o -> p hc o", p=128),
                    ).then_inc(ldg, 16)

            @b4.tensor
            def _(tensor):
                tensor.wait_ge(v_sfin, 1)
                for hc in range(16):
                    for jj in range(2):
                        idx = 2 * hc + jj
                        tensor.wait_ge(q_rl, 16 * (hc + 2))
                        if idx > 0:
                            tensor.wait_ge(v_bc, idx)
                        nc.tensor.matmul(
                            psB[:, 0:512], ones_f1[:],
                            rl_buf[0:1, (hc % 2) * TOK + jj * 512:(hc % 2) * TOK + (jj + 1) * 512],
                            start=True, stop=True).then_inc(t_bc, 1)
                tensor.wait_ge(v_yn, 32)
                for oc in range(4):
                    tensor.wait_ge(ldg, 16 * (oc + 2))
                    for tt in range(8):
                        a = 8 * oc + tt
                        if a >= 2:
                            tensor.wait_ge(v_x2, a - 1)
                        for hc in range(16):
                            nc.tensor.matmul(
                                psacc[:, (a % 2) * 512:(a % 2 + 1) * 512],
                                y_all[:, hc * TOK + tt * 128:hc * TOK + (tt + 1) * 128],
                                wo_bf[:, (oc % 2) * 8192 + hc * 512:(oc % 2) * 8192 + (hc + 1) * 512],
                                start=(hc == 0), stop=False)
                        nc.tensor.matmul(
                            psacc[:, (a % 2) * 512:(a % 2 + 1) * 512],
                            ones_bf[:], bo_bf[0:1, oc * 512:(oc + 1) * 512],
                            start=False, stop=True).then_inc(t_wo, 1)

            @b4.vector
            def _(vector):
                vector.wait_ge(ld_y, 16)
                for hc in range(16):
                    for jj in range(2):
                        idx = 2 * hc + jj
                        vector.wait_ge(t_bc, idx + 1)
                        nc.vector.tensor_copy(rlb[:], psB[:, 0:512]).then_inc(v_bc, 1)
                        nc.vector.tensor_tensor(
                            y_all[:, hc * TOK + jj * 512:hc * TOK + (jj + 1) * 512],
                            y_all[:, hc * TOK + jj * 512:hc * TOK + (jj + 1) * 512],
                            rlb[:], ALU.mult).then_inc(v_yn, 1)
                for oc in range(4):
                    for tt in range(8):
                        a = 8 * oc + tt
                        vector.wait_ge(t_wo, a + 1)
                        nc.vector.tensor_copy(
                            x2[:, tt * C + oc * 512:tt * C + (oc + 1) * 512],
                            psacc[:, (a % 2) * 512:(a % 2 + 1) * 512],
                        ).then_inc(v_x2, 1)
                for tt in range(8):
                    vector.wait_ge(ld_xo, 16 * (tt + 1))
                    if tt == 0:
                        vector.wait_ge(v_x2, 32)
                    nc.vector.tensor_add(
                        x2[:, tt * C:(tt + 1) * C],
                        xo[:, (tt % 2) * C:(tt % 2 + 1) * C],
                        x2[:, tt * C:(tt + 1) * C]).then_inc(v_x2b, 1)

        xq_es = contextlib.ExitStack()
        xq_fm = xq_es.enter_context(nc.sbuf_tensor("xq_fm", [128, 16 * TOK], BF16))
        with (
            nc.sbuf_tensor("zq", [128, C], F32) as zq,
            nc.sbuf_tensor("xqt", [128, C], BF16) as xqt,
            nc.sbuf_tensor("xmax", [128, 1], F32) as xmax,
            nc.sbuf_tensor("xmin", [128, 1], F32) as xmin,
            nc.sbuf_tensor("s5sum", [128, 1], F32) as s5sum,
            nc.sbuf_tensor("s5sq", [128, 1], F32) as s5sq,
            nc.sbuf_tensor("u5", [128, 1], F32) as u5,
            nc.sbuf_tensor("m5", [128, 1], F32) as m5,
            nc.sbuf_tensor("r05", [128, 1], F32) as r05,
            nc.Block() as b5,
        ):
            @b5.vector
            def _(vector):
                vector.wait_ge(v_x2b, 8)
                vector.wait_ge(v_sfin, 1)
                for tt in range(8):
                    sl = slice(tt * C, (tt + 1) * C)
                    nc.vector.reduce_sum(s5sum[:], x2[:, sl], axis=AX)
                    nc.vector.scalar_tensor_tensor(zq[:], x2[:, sl], 1.0, x2[:, sl],
                                                   op0=ALU.mult, op1=ALU.mult,
                                                   accum_out=s5sq[:])
                    vector.drain()
                    nc.vector.tensor_scalar_mul(t_b[:], s5sum[:], 1.0 / C)
                    vector.drain()
                    nc.vector.scalar_tensor_tensor(t_c[:], t_b[:], -1.0, t_b[:],
                                                   op0=ALU.mult, op1=ALU.mult)
                    vector.drain()
                    nc.vector.scalar_tensor_tensor(t_a[:], s5sq[:], 1.0 / C, t_c[:],
                                                   op0=ALU.mult, op1=ALU.add)
                    vector.drain()
                    nc.vector.tensor_scalar_add(u5[:], t_a[:], EPS)
                    vector.drain()
                    nc.vector.tensor_copy(m5[:], t_b[:])
                    nc.vector.reciprocal(t_b[:], u5[:]).then_inc(v_st5, 1)
                    nc.vector.reduce_max(xmax[:], x2[:, sl], axis=AX)
                    nc.vector.tensor_reduce(xmin[:], x2[:, sl], axis=AX, op=ALU.min)
                    vector.drain()
                    vector.wait_ge(s_r5, tt + 1)
                    nc.vector.tensor_mul(t_a[:], r05[:], r05[:])
                    vector.drain()
                    nc.vector.tensor_mul(t_a[:], t_a[:], u5[:])
                    vector.drain()
                    nc.vector.tensor_scalar(t_a[:], t_a[:], -0.5, 1.5,
                                            op0=ALU.mult, op1=ALU.add)
                    vector.drain()
                    nc.vector.tensor_mul(r5[:], r05[:], t_a[:])
                    vector.drain()
                    nc.vector.tensor_sub(t_a[:], xmax[:], m5[:])
                    vector.drain()
                    nc.vector.tensor_sub(t_c[:], m5[:], xmin[:])
                    vector.drain()
                    nc.vector.tensor_max(t_a[:], t_a[:], t_c[:])
                    vector.drain()
                    nc.vector.tensor_mul(t_a[:], t_a[:], r5[:])
                    vector.drain()
                    nc.vector.tensor_scalar_max(t_a[:], t_a[:], EPS)
                    vector.drain()
                    nc.vector.reciprocal(t_c[:], t_a[:])
                    vector.drain()
                    nc.vector.tensor_scalar_mul(s1c[:], t_c[:], 127.0)
                    vector.drain()
                    nc.vector.tensor_mul(asc[:], r5[:], s1c[:])
                    vector.drain()
                    nc.vector.scalar_tensor_tensor(abi[:], m5[:], -1.0, asc[:],
                                                   op0=ALU.mult, op1=ALU.mult)
                    vector.drain()
                    nc.vector.scalar_tensor_tensor(au_all[:, tt:tt + 1], t_a[:],
                                                   1.0 / 127.0, s_cols[:, 1:2],
                                                   op0=ALU.mult, op1=ALU.mult)
                    vector.drain()
                    vector.nop().then_inc(v_ab5, 1)
                    vector.wait_ge(s_z5, tt + 1)
                    nc.vector.tensor_scalar(xqt[:], zq[:], cmag[:, 0:1], negc[:, 0:1],
                                            op0=ALU.add, op1=ALU.add).then_inc(v_rq5, 1)
                    vector.wait_ge(t_tr5, tt + 1)
                    nc.vector.tensor_copy(
                        xq_fm.ap().rearrange("p (hc t) -> p hc t", hc=16)[:, :, tt * 128:(tt + 1) * 128],
                        ps_tr.ap().rearrange("p (hc t) -> p hc t", hc=16),
                    ).then_inc(v_xq, 1)

            @b5.scalar
            def _(scalar):
                for tt in range(8):
                    scalar.wait_ge(v_st5, tt + 1)
                    nc.scalar.activation(r05[:], t_b[:], AF.Sqrt).then_inc(s_r5, 1)
                    scalar.wait_ge(v_ab5, tt + 1)
                    if tt >= 1:
                        scalar.wait_ge(v_rq5, tt)
                    nc.scalar.activation(zq[:], x2[:, tt * C:(tt + 1) * C],
                                         AF.Identity, bias=abi[:, 0:1],
                                         scale=asc[:, 0:1]).then_inc(s_z5, 1)

            @b5.tensor
            def _(tensor):
                for tt in range(8):
                    tensor.wait_ge(v_rq5, tt + 1)
                    if tt > 0:
                        tensor.wait_ge(v_xq, tt)
                    mm = None
                    for i in range(16):
                        mm = nc.tensor.transpose(ps_tr[:, i * 128:(i + 1) * 128],
                                                 xqt[:, i * 128:(i + 1) * 128],
                                                 ident[:])
                    mm.then_inc(t_tr5, 1)

            @b5.sync
            def _(sync):
                sync.wait_ge(s_z5, 8)
                sync.wait_ge(v_xq, 8)
                sync.dma_start(out=x2_dram, in_=x2[:]).then_inc(q_x2s, 16)
        x2_es.close()

        h_es = contextlib.ExitStack()
        h_sb = h_es.enter_context(nc.sbuf_tensor("h_sb", [128, 8 * I_FF], FP16, side="right"))
        with (
            nc.sbuf_tensor("wt6", [128, 2 * C], F32) as wt6,
            nc.sbuf_tensor("q1", [128, C], F32) as q1,
            nc.sbuf_tensor("q2", [128, C], F32) as q2,
            nc.sbuf_tensor("tnat", [128, C], BF16) as tnat,
            nc.sbuf_tensor("tup", [128, 16 * 512], BF16) as tup,
            nc.sbuf_tensor("pscr", [128, 512], F32) as pscr,
            nc.Block() as b6,
        ):
            @b6.sync
            def _(sync):
                sync.wait_ge(v_x2b, 8)
                sync.wait_ge(v_xq, 8)
                for j in range(0, 32, 2):
                    if j >= 2:
                        sync.wait_ge(v_q6, j - 1)
                    sync.dma_start(
                        out=wt6[:, 0:C],
                        in_=wup_d[j * 128:(j + 1) * 128, :],
                    ).then_inc(ldw, 16)

            @b6.gpsimd
            def _(gpsimd):
                gpsimd.wait_ge(v_x2b, 8)
                gpsimd.wait_ge(v_xq, 8)
                for j in range(1, 32, 2):
                    if j >= 2:
                        gpsimd.wait_ge(v_q6, j - 1)
                    gpsimd.dma_start(
                        out=wt6[:, C:2 * C],
                        in_=wup_d[j * 128:(j + 1) * 128, :],
                    ).then_inc(ldwg, 16)

            @b6.scalar
            def _(scalar):
                scalar.wait_ge(v_sfin, 1)
                scalar.wait_ge(v_pr, 1)
                scalar.wait_ge(q_x2s, 16)
                for Ig in range(8):
                    for i4 in range(4):
                        j = 4 * Ig + i4
                        scalar.wait_ge(ldw if j % 2 == 0 else ldwg,
                                       16 * (j // 2 + 1))
                        if j >= 1:
                            scalar.wait_ge(v_q6, j)
                        nc.scalar.activation(q1[:], wt6[:, (j % 2) * C:(j % 2 + 1) * C],
                                             AF.Identity, bias=cmag[:, 0:1],
                                             scale=s_cols[:, 0:1]).then_inc(s_q6, 1)
                    for tt in range(8):
                        g6 = 8 * Ig + tt
                        scalar.wait_ge(t_up, g6 + 1)
                        nc.scalar.activation(pscr[:], psacc[:, (g6 % 2) * 512:(g6 % 2 + 1) * 512],
                                             AF.Relu, scale=au_all[:, tt:tt + 1])
                        nc.scalar.activation(
                            h_sb[:, tt * I_FF + Ig * 512:tt * I_FF + (Ig + 1) * 512],
                            pscr[:], AF.Square).then_inc(s_h6, 1)

            @b6.vector
            def _(vector):
                for Ig in range(8):
                    for i4 in range(4):
                        j = 4 * Ig + i4
                        vector.wait_ge(s_q6, j + 1)
                        nc.vector.tensor_scalar(q2[:], q1[:], negc[:, 0:1], 1.0,
                                                op0=ALU.add, op1=ALU.min)
                        nc.vector.tensor_scalar(tnat[:], q2[:], -1.0, None,
                                                op0=ALU.max).then_inc(v_q6, 1)
                        vector.wait_ge(t_tw6, j + 1)
                        pst = ps_tr if j % 2 == 0 else ps_tr2
                        nc.vector.tensor_copy(
                            tup.ap().rearrange("p (hc i) -> p hc i", hc=16)[
                                :, :, i4 * 128:(i4 + 1) * 128],
                            pst.ap().rearrange("p (hc i) -> p hc i", hc=16),
                        ).then_inc(v_tw6, 1)
                    for tt in range(8):
                        g6 = 8 * Ig + tt
                        vector.wait_ge(s_h6, g6 + 1)
                        nc.vector.reduce_max(pmax_all[:, tt * 8 + Ig:tt * 8 + Ig + 1],
                                             h_sb[:, tt * I_FF + Ig * 512:tt * I_FF + (Ig + 1) * 512],
                                             axis=AX).then_inc(v_hm, 1)

            @b6.tensor
            def _(tensor):
                for Ig in range(8):
                    for i4 in range(4):
                        j = 4 * Ig + i4
                        tensor.wait_ge(v_q6, j + 1)
                        if j >= 2:
                            tensor.wait_ge(v_tw6, j - 1)
                        pst = ps_tr if j % 2 == 0 else ps_tr2
                        mm = None
                        for blk in range(16):
                            mm = nc.tensor.transpose(
                                pst[:, blk * 128:(blk + 1) * 128],
                                tnat[:, blk * 128:(blk + 1) * 128],
                                ident[:])
                        mm.then_inc(t_tw6, 1)
                    tensor.wait_ge(v_tw6, 4 * (Ig + 1))
                    for tt in range(8):
                        g6 = 8 * Ig + tt
                        if g6 >= 2:
                            tensor.wait_ge(s_h6, g6 - 1)
                        mm = None
                        for hc in range(16):
                            mm = nc.tensor.matmul(
                                psacc[:, (g6 % 2) * 512:(g6 % 2 + 1) * 512],
                                xq_fm[:, hc * TOK + tt * 128:hc * TOK + (tt + 1) * 128],
                                tup[:, hc * 512:(hc + 1) * 512],
                                start=(hc == 0), stop=(hc == 15))
                        mm.then_inc(t_up, 1)

        xq_es.close()
        hq_es = contextlib.ExitStack()
        hq_fm = hq_es.enter_context(nc.sbuf_tensor("hq_fm", [128, 32 * TOK], BF16))
        with (
            nc.sbuf_tensor("hq1", [128, 512], F32) as hq1,
            nc.sbuf_tensor("hqt", [128, 512], BF16) as hqt,
            nc.Block() as b7,
        ):
            @b7.vector
            def _(vector):
                vector.wait_ge(s_h6, 64)
                vector.wait_ge(v_hm, 64)
                for tt in range(8):
                    nc.vector.reduce_max(t_a[:], pmax_all[:, tt * 8:(tt + 1) * 8], axis=AX)
                    vector.drain()
                    nc.vector.tensor_scalar_max(t_a[:], t_a[:], EPS)
                    vector.drain()
                    nc.vector.reciprocal(t_b[:], t_a[:])
                    vector.drain()
                    nc.vector.tensor_scalar_mul(s2c[:], t_b[:], 127.0)
                    vector.drain()
                    nc.vector.scalar_tensor_tensor(ad_all[:, tt:tt + 1], t_a[:],
                                                   1.0 / 127.0, s_cols[:, 3:4],
                                                   op0=ALU.mult, op1=ALU.mult)
                    vector.drain()
                    vector.nop().then_inc(v_st7, 1)
                    for Ig in range(8):
                        idx = 8 * tt + Ig
                        nc.vector.tensor_scalar(
                            hq1[:], h_sb[:, tt * I_FF + Ig * 512:tt * I_FF + (Ig + 1) * 512],
                            s2c[:, 0:1], cmag[:, 0:1], op0=ALU.mult, op1=ALU.add)
                        nc.vector.tensor_scalar(hqt[:], hq1[:], negc[:, 0:1], None,
                                                op0=ALU.add).then_inc(v_rq7, 1)
                        vector.wait_ge(t_tr7, idx + 1)
                        pst = ps_tr if idx % 2 == 0 else ps_tr2
                        nc.vector.tensor_copy(
                            hq_fm.ap().rearrange("p (ic t) -> p ic t", ic=32)[
                                :, Ig * 4:(Ig + 1) * 4, tt * 128:(tt + 1) * 128],
                            pst.ap().rearrange("p (ic t) -> p ic t", ic=16)[:, 0:4],
                        ).then_inc(v_hq, 1)

            @b7.tensor
            def _(tensor):
                for tt in range(8):
                    for Ig in range(8):
                        idx = 8 * tt + Ig
                        tensor.wait_ge(v_rq7, idx + 1)
                        if idx >= 2:
                            tensor.wait_ge(v_hq, idx - 1)
                        pst = ps_tr if idx % 2 == 0 else ps_tr2
                        mm = None
                        for i in range(4):
                            mm = nc.tensor.transpose(
                                pst[:, i * 128:(i + 1) * 128],
                                hqt[:, i * 128:(i + 1) * 128],
                                ident[:])
                        mm.then_inc(t_tr7, 1)

        h_es.close()
        x2b_es = contextlib.ExitStack()
        x2 = x2b_es.enter_context(nc.sbuf_tensor("x2b", [128, 8 * C], F32, side="right"))
        with (
            nc.sbuf_tensor("wt8", [128, 2 * C], F32) as wt8,
            nc.sbuf_tensor("d1", [128, C], F32) as d1,
            nc.sbuf_tensor("d2", [128, C], F32) as d2,
            nc.sbuf_tensor("dnat", [128, C], BF16) as dnat,
            nc.sbuf_tensor("tdn", [128, 32 * 512], BF16) as tdn,
            nc.sbuf_tensor("pscr8", [128, 512], F32) as pscr8,
            nc.Block() as b8,
        ):
            @b8.sync
            def _(sync):
                sync.wait_ge(v_hq, 64)
                sync.dma_start(out=x2[:], in_=x2_dram).then_inc(q_x2r, 16)
                for j in range(0, 32, 2):
                    if j >= 2:
                        sync.wait_ge(s_q8, j - 1)
                    sync.dma_start(
                        out=wt8[:, 0:C],
                        in_=wdn_d[(j // 2) * 128:(j // 2 + 1) * 128, 0:2048],
                    ).then_inc(ldw, 16)
                sync.wait_ge(st_out, 16 * 4)

            @b8.gpsimd
            def _(gpsimd):
                gpsimd.wait_ge(v_hq, 64)
                for j in range(1, 32, 2):
                    if j >= 2:
                        gpsimd.wait_ge(s_q8, j - 1)
                    gpsimd.dma_start(
                        out=wt8[:, C:2 * C],
                        in_=wdn_d[(j // 2) * 128:(j // 2 + 1) * 128, 2048:4096],
                    ).then_inc(ldwg, 16)

            @b8.scalar
            def _(scalar):
                for Hg in range(4):
                    for h4 in range(4):
                        for half in range(2):
                            j = 8 * Hg + 2 * h4 + half
                            scalar.wait_ge(ldw if j % 2 == 0 else ldwg,
                                           16 * (16 + j // 2 + 1))
                            if j >= 1:
                                scalar.wait_ge(v_q8, j)
                            nc.scalar.activation(d1[:], wt8[:, (j % 2) * C:(j % 2 + 1) * C],
                                                 AF.Identity, bias=cmag[:, 0:1],
                                                 scale=s_cols[:, 2:3]).then_inc(s_q8, 1)
                    for tt in range(8):
                        g8 = 8 * Hg + tt
                        scalar.wait_ge(t_dn, g8 + 1)
                        if g8 > 0:
                            scalar.wait_ge(v_o8, g8)
                        nc.scalar.activation(pscr8[:], psacc[:, (g8 % 2) * 512:(g8 % 2 + 1) * 512],
                                             AF.Identity, scale=ad_all[:, tt:tt + 1]
                                             ).then_inc(s_o8, 1)
                    scalar.wait_ge(v_o8, 8 * (Hg + 1))
                    scalar.dma_start(
                        out=out_d.rearrange("(tt p) c -> p tt c", p=128)[
                            :, :, Hg * 512:(Hg + 1) * 512],
                        in_=x2.ap().rearrange("p (tt c) -> p tt c", tt=8)[
                            :, :, Hg * 512:(Hg + 1) * 512],
                    ).then_inc(st_out, 16)

            @b8.vector
            def _(vector):
                for Hg in range(4):
                    for h4 in range(4):
                        for half in range(2):
                            j = 8 * Hg + 2 * h4 + half
                            vector.wait_ge(s_q8, j + 1)
                            nc.vector.tensor_scalar(d2[:], d1[:], negc[:, 0:1], 1.0,
                                                    op0=ALU.add, op1=ALU.min)
                            nc.vector.tensor_scalar(dnat[:], d2[:], -1.0, None,
                                                    op0=ALU.max).then_inc(v_q8, 1)
                            vector.wait_ge(t_tw8, j + 1)
                            pst = ps_tr if j % 2 == 0 else ps_tr2
                            nc.vector.tensor_copy(
                                tdn.ap().rearrange("p (ic hh) -> p ic hh", ic=32)[
                                    :, half * 16:(half + 1) * 16,
                                    h4 * 128:(h4 + 1) * 128],
                                pst.ap().rearrange("p (ic hh) -> p ic hh", ic=16),
                            ).then_inc(v_tw8, 1)
                    for tt in range(8):
                        g8 = 8 * Hg + tt
                        vector.wait_ge(s_o8, g8 + 1)
                        if g8 == 0:
                            vector.wait_ge(q_x2r, 16)
                        nc.vector.tensor_add(
                            x2[:, tt * C + Hg * 512:tt * C + (Hg + 1) * 512],
                            pscr8[:],
                            x2[:, tt * C + Hg * 512:tt * C + (Hg + 1) * 512]
                        ).then_inc(v_o8, 1)

            @b8.tensor
            def _(tensor):
                for Hg in range(4):
                    if Hg > 0:
                        tensor.wait_ge(s_o8, 8 * Hg)
                    for h4 in range(4):
                        for half in range(2):
                            j = 8 * Hg + 2 * h4 + half
                            tensor.wait_ge(v_q8, j + 1)
                            if j >= 2:
                                tensor.wait_ge(v_tw8, j - 1)
                            pst = ps_tr if j % 2 == 0 else ps_tr2
                            mm = None
                            for blk in range(16):
                                mm = nc.tensor.transpose(
                                    pst[:, blk * 128:(blk + 1) * 128],
                                    dnat[:, blk * 128:(blk + 1) * 128],
                                    ident[:])
                            mm.then_inc(t_tw8, 1)
                    tensor.wait_ge(v_tw8, 8 * (Hg + 1))
                    for tt in range(8):
                        g8 = 8 * Hg + tt
                        if g8 >= 2:
                            tensor.wait_ge(s_o8, g8 - 1)
                        mm = None
                        for ic in range(32):
                            mm = nc.tensor.matmul(
                                psacc[:, (g8 % 2) * 512:(g8 % 2 + 1) * 512],
                                hq_fm[:, ic * TOK + tt * 128:ic * TOK + (tt + 1) * 128],
                                tdn[:, ic * 512:(ic + 1) * 512],
                                start=(ic == 0), stop=(ic == 31))
                        mm.then_inc(t_dn, 1)
        x2b_es.close()
        hq_es.close()

    return nc


_NC = None


def _get_program():
    global _NC
    if _NC is None:
        _NC = build_fused()
    return _NC


def _make_masks():
    p = np.arange(128)[:, None]
    f = np.arange(512)[None, :]
    m = np.zeros((128, 4 * 512), dtype=BF)
    for kcl in range(4):
        m[:, kcl * 512:(kcl + 1) * 512] = (f >= kcl * 128 + p).astype(BF)
    return m


def kernel(L0_attn_x, L0_attn_Wq, L0_attn_bq, L0_attn_Wk, L0_attn_bk,
           L0_attn_Wv, L0_attn_bv, L0_attn_Wo, L0_attn_bo,
           L0_ffn_w_up, L0_ffn_w_down):
    x = np.ascontiguousarray(np.asarray(L0_attn_x, dtype=np.float32))
    Wq = np.ascontiguousarray(np.asarray(L0_attn_Wq, dtype=np.float32))
    Wk = np.ascontiguousarray(np.asarray(L0_attn_Wk, dtype=np.float32))
    Wv = np.ascontiguousarray(np.asarray(L0_attn_Wv, dtype=np.float32))
    bq = np.asarray(L0_attn_bq, dtype=np.float32)
    bk = np.asarray(L0_attn_bk, dtype=np.float32)
    bv = np.asarray(L0_attn_bv, dtype=np.float32)
    Wo = np.ascontiguousarray(np.asarray(L0_attn_Wo, dtype=np.float32))
    bo = np.asarray(L0_attn_bo, dtype=np.float32).reshape(1, C)
    w_up = np.ascontiguousarray(np.asarray(L0_ffn_w_up, dtype=np.float32))
    w_down = np.ascontiguousarray(np.asarray(L0_ffn_w_down, dtype=np.float32))

    masks = _make_masks()
    ident = np.eye(128, dtype=BF)
    x_both = np.ascontiguousarray(x.reshape(2 * T, C))

    nc = _get_program()

    in_maps = []
    for c in range(8):
        b, q = c // 4, c % 4
        h0, h1 = 2 * c, 2 * c + 1
        wq_s = np.ascontiguousarray(np.stack([Wq[h0], Wq[h1], Wq[h0], Wq[h1]]))
        wk_s = np.ascontiguousarray(np.stack([Wk[h0], Wk[h1], Wk[h0], Wk[h1]]))
        wv_s = np.ascontiguousarray(np.stack([Wv[h0], Wv[h1], Wv[h0], Wv[h1]]))
        b_cols = np.zeros((128, 12), dtype=np.float32)
        for sl in range(4):
            hh = 2 * c + sl % 2
            b_cols[:, sl] = bq[hh]
            b_cols[:, 4 + sl] = bk[hh]
            b_cols[:, 8 + sl] = bv[hh]
        in_maps.append({
            "x_both": x_both,
            "wq": wq_s, "wk": wk_s, "wv": wv_s,
            "b_cols": b_cols,
            "masks": masks,
            "ident": ident,
            "x_own": np.ascontiguousarray(x[b, q * TOK:(q + 1) * TOK]),
            "Wo": Wo,
            "bo": bo,
            "w_up": w_up,
            "w_down": w_down,
        })
    res = run_bass_kernel_spmd(nc, in_maps, core_ids=list(range(8)))

    out = np.empty((2, T, C), dtype=np.float32)
    for c in range(8):
        b, q = c // 4, c % 4
        out[b, q * TOK:(q + 1) * TOK] = res.results[c]["out"]
    return out


# version 10 (final)
# speedup vs baseline: 1.0214x; 1.0214x over previous
import contextlib
import os

os.environ.setdefault("TRN_ALLOW_ICI_ALL_TO_ALL", "1")

import numpy as np
import ml_dtypes

import concourse.bass as bass
import concourse.mybir as mybir
from concourse.bass_utils import run_bass_kernel_spmd

F32 = mybir.dt.float32
BF16 = mybir.dt.bfloat16
FP16 = mybir.dt.float16
AF = mybir.ActivationFunctionType
ALU = mybir.AluOpType
AX = mybir.AxisListType.X

BF = ml_dtypes.bfloat16

T = 4096
C = 2048
D = 128
NHMY = 4
TOK = 1024
I_FF = 4096
SCALE = float(np.float32(1.0 / np.sqrt(128.0)))
C_MAGIC = 12582912.0
K54 = 18014398509481984.0
EPS = 1e-5


def ccum(t8):
    return 2 * t8 * (t8 + 1)


CPH = ccum(8)
TA_LN = [1, 3, 5, 9, 13, 19, 25, 32]
TT_LN = [max(t - 1, 0) for t in TA_LN]


def build_fused():
    nc = bass.Bass("TRN2", target_bir_lowering=False, debug=False)

    x = nc.dram_tensor("x_both", [2 * T, C], F32, kind="ExternalInput").ap()
    wq = nc.dram_tensor("wq", [NHMY, C, D], F32, kind="ExternalInput").ap()
    wk = nc.dram_tensor("wk", [NHMY, C, D], F32, kind="ExternalInput").ap()
    wv = nc.dram_tensor("wv", [NHMY, C, D], F32, kind="ExternalInput").ap()
    bcols = nc.dram_tensor("b_cols", [128, 12], F32, kind="ExternalInput").ap()
    masks_d = nc.dram_tensor("masks", [128, 4 * 512], BF16, kind="ExternalInput").ap()
    ident_d = nc.dram_tensor("ident", [128, 128], BF16, kind="ExternalInput").ap()
    x_own = nc.dram_tensor("x_own", [TOK, C], F32, kind="ExternalInput").ap()
    Wo_d = nc.dram_tensor("Wo", [C, C], F32, kind="ExternalInput").ap()
    bo_d = nc.dram_tensor("bo", [1, C], F32, kind="ExternalInput").ap()
    wup_d = nc.dram_tensor("w_up", [I_FF, C], F32, kind="ExternalInput").ap()
    wdn_d = nc.dram_tensor("w_down", [C, I_FF], F32, kind="ExternalInput").ap()

    out_d = nc.dram_tensor("out", [TOK, C], F32, kind="ExternalOutput").ap()

    cc_y_in = nc.dram_tensor("cc_y_in", [2048, TOK], BF16, kind="Internal").ap()
    cc_l_in = nc.dram_tensor("cc_l_in", [16, TOK], F32, kind="Internal").ap()
    cc_y_out = nc.dram_tensor("cc_y_out", [2048, TOK], BF16, kind="Internal").ap()
    cc_l_out = nc.dram_tensor("cc_l_out", [16, TOK], F32, kind="Internal").ap()
    rl_dram = nc.dram_tensor("rl_scratch", [1, 16 * TOK], F32, kind="Internal").ap()
    x2_dram = nc.dram_tensor("x2_scratch", [128, 8 * C], F32, kind="Internal").ap()

    NT1 = 32

    oes = contextlib.ExitStack()
    with oes:
        ident = oes.enter_context(nc.sbuf_tensor("ident_sb", [128, 128], BF16))
        absacc = oes.enter_context(nc.sbuf_tensor("absacc", [128, 64], F32))
        stat1 = oes.enter_context(nc.sbuf_tensor("stat1", [128, 1], F32))
        stat2 = oes.enter_context(nc.sbuf_tensor("stat2", [128, 1], F32))
        ccs = oes.enter_context(nc.semaphore("ccs"))
        ld_ws = oes.enter_context(nc.semaphore("ld_ws"))
        ld_wsg = oes.enter_context(nc.semaphore("ld_wsg"))
        v_abs = oes.enter_context(nc.semaphore("v_abs"))
        es = contextlib.ExitStack()
        LNx = es.enter_context(nc.sbuf_tensor("LNx", [128, 16 * T], BF16))
        masks = es.enter_context(nc.sbuf_tensor("masks_sb", [128, 4 * 512], BF16))
        bcol = es.enter_context(nc.sbuf_tensor("bcol", [128, 12], F32))
        ones_bf = es.enter_context(nc.sbuf_tensor("ones_bfA", [128, 128], BF16))
        wq_sb = es.enter_context(nc.sbuf_tensor("wq_sb", [128, 2048], BF16))
        wk_sb = es.enter_context(nc.sbuf_tensor("wk_sb", [128, 2048], BF16))
        wv_sb = es.enter_context(nc.sbuf_tensor("wv_sb", [128, 2048], BF16))
        K_fm = es.enter_context(nc.sbuf_tensor("K_fm", [128, T], BF16))
        Q_fm = es.enter_context(nc.sbuf_tensor("Q_fm", [128, T], BF16))
        V_tm = es.enter_context(nc.sbuf_tensor("V_tm", [128, T], BF16))
        y_sb = es.enter_context(nc.sbuf_tensor("y_sb", [128, T], BF16))
        P_sb = es.enter_context(nc.sbuf_tensor("P_sb", [128, 4 * 512], BF16))
        l_tile = es.enter_context(nc.sbuf_tensor("l_tile", [1, 2 * 512], F32))
        xt = es.enter_context(nc.sbuf_tensor("xt", [128, 2 * C], F32, side="right"))
        ln_tm = es.enter_context(nc.sbuf_tensor("ln_tm", [128, C], BF16))
        scrapf = es.enter_context(nc.sbuf_tensor("scrapf", [128, 512], F32, side="right"))
        ssum = es.enter_context(nc.sbuf_tensor("ssum", [128, 2], F32))
        ssq = es.enter_context(nc.sbuf_tensor("ssq", [128, 8], F32))
        m_c = es.enter_context(nc.sbuf_tensor("m_c", [128, 1], F32))
        nmsq = es.enter_context(nc.sbuf_tensor("nmsq", [128, 1], F32))
        v_c = es.enter_context(nc.sbuf_tensor("v_c", [128, 1], F32))
        u_c = es.enter_context(nc.sbuf_tensor("u_c", [128, 2], F32))
        msq = es.enter_context(nc.sbuf_tensor("msq", [128, 1], F32))
        sn1 = es.enter_context(nc.sbuf_tensor("sn1", [128, 1], F32))
        sn2 = es.enter_context(nc.sbuf_tensor("sn2", [128, 1], F32))
        sn3 = es.enter_context(nc.sbuf_tensor("sn3", [128, 1], F32))
        epsc = es.enter_context(nc.sbuf_tensor("epsc", [128, 1], F32))
        c15 = es.enter_context(nc.sbuf_tensor("c15", [128, 1], F32))
        rv = es.enter_context(nc.sbuf_tensor("rv", [128, 2], F32))
        r0c = es.enter_context(nc.sbuf_tensor("r0c", [128, 1], F32))
        nta = es.enter_context(nc.sbuf_tensor("nta", [128, 1], F32))
        r_col = es.enter_context(nc.sbuf_tensor("r_col", [128, 1], F32))
        negmr = es.enter_context(nc.sbuf_tensor("negmr", [128, 1], F32))
        ps_tr = es.enter_context(nc.psum_tensor("ps_trA", [128, 2048], F32))
        pstb = es.enter_context(nc.psum_tensor("pstb", [128, 2048], BF16))
        psY = es.enter_context(nc.psum_tensor("psY", [128, 512], F32))
        psL = es.enter_context(nc.psum_tensor("psL", [128, 512], F32))
        ld_c = oes.enter_context(nc.semaphore("ld_c"))
        ld_xe = oes.enter_context(nc.semaphore("ld_xA0"))
        ld_xod = oes.enter_context(nc.semaphore("ld_xA1"))
        ld_w = oes.enter_context(nc.semaphore("ld_w"))
        v_stat = oes.enter_context(nc.semaphore("v_stat"))
        s_r = oes.enter_context(nc.semaphore("s_r"))
        v_nm = oes.enter_context(nc.semaphore("v_nm"))
        s_ln = oes.enter_context(nc.semaphore("s_ln"))
        t_tr = oes.enter_context(nc.semaphore("t_tr"))
        v_ev = oes.enter_context(nc.semaphore("v_ev"))
        t_qkv = oes.enter_context(nc.semaphore("t_qkv"))
        s_ev = oes.enter_context(nc.semaphore("s_ev"))
        t_vtr = oes.enter_context(nc.semaphore("t_vtr"))
        v_vt = oes.enter_context(nc.semaphore("v_vt"))
        t_s = oes.enter_context(nc.semaphore("t_s"))
        s_exp = oes.enter_context(nc.semaphore("s_exp"))
        v_mk = oes.enter_context(nc.semaphore("v_mk"))
        t_pv = oes.enter_context(nc.semaphore("t_pv"))
        v_fin = oes.enter_context(nc.semaphore("v_fin"))
        q_l = oes.enter_context(nc.semaphore("q_l"))
        st_y = oes.enter_context(nc.semaphore("st_y"))
        v_on = oes.enter_context(nc.semaphore("v_on"))
        s_u = oes.enter_context(nc.semaphore("s_u"))
        v_rv = oes.enter_context(nc.semaphore("v_rv"))
        block = es.enter_context(nc.Block())

        @block.sync
        def _(sync):
            sync.dma_start(out=masks[:], in_=masks_d).then_inc(ld_c, 16)
            sync.dma_start(out=ident[:], in_=ident_d).then_inc(ld_c, 16)
            sync.dma_start(out=bcol[:], in_=bcols).then_inc(ld_c, 16)
            for tg in range(2 * NT1):
                if tg >= 2:
                    sync.wait_ge(v_stat, tg - 1)
                    sync.wait_ge(s_ln, tg - 1)
                sync.dma_start(
                    out=xt[:, (tg % 2) * C:(tg % 2 + 1) * C],
                    in_=x[tg * 128:(tg + 1) * 128, :],
                ).then_inc(ld_xe if tg % 2 == 0 else ld_xod, 16)
            sync.wait_ge(s_ln, 64)
            for j in range(0, 64, 2):
                if j >= 2:
                    sync.wait_ge(v_abs, j - 1)
                if j < 32:
                    src_w = wup_d[j * 128:(j + 1) * 128, :]
                else:
                    jj = j - 32
                    src_w = wdn_d[(jj // 2) * 128:(jj // 2 + 1) * 128,
                                  (jj % 2) * 2048:(jj % 2 + 1) * 2048]
                sync.dma_start(out=xt[:, 0:C],
                               in_=src_w).then_inc(ld_ws, 16)

        @block.gpsimd
        def _(gpsimd):
            for h in range(NHMY):
                if h > 0:
                    gpsimd.wait_ge(t_qkv, 24 * h)
                for (dst, srcw) in ((wq_sb, wq), (wk_sb, wk), (wv_sb, wv)):
                    gpsimd.dma_start(
                        out=dst.ap().rearrange("p (ch d) -> p ch d", ch=16),
                        in_=srcw[h].rearrange("(ch p) d -> p ch d", p=128),
                    ).then_inc(ld_w, 16)
            gpsimd.wait_ge(s_ln, 64)
            for j in range(1, 64, 2):
                if j >= 2:
                    gpsimd.wait_ge(v_abs, j - 1)
                if j < 32:
                    src_w = wup_d[j * 128:(j + 1) * 128, :]
                else:
                    jj = j - 32
                    src_w = wdn_d[(jj // 2) * 128:(jj // 2 + 1) * 128,
                                  (jj % 2) * 2048:(jj % 2 + 1) * 2048]
                gpsimd.dma_start(out=xt[:, C:2 * C],
                                 in_=src_w).then_inc(ld_wsg, 16)

        def ln_v_unit(vector, tl, off, emit_evac=True):
            tg = off + tl
            c = tg % 2
            if tl >= 1:
                ln_v_recip(vector, tg - 1)
            if emit_evac and tl >= 2:
                ln_v_evac(vector, tg - 2, off)
            vector.wait_ge(ld_xe if tg % 2 == 0 else ld_xod,
                           16 * (tg // 2 + 1))
            nc.vector.reduce_sum(ssum[:, c:c + 1], xt[:, c * C:(c + 1) * C],
                                 axis=AX)
            for qq in range(4):
                sl = slice(c * C + qq * 512, c * C + (qq + 1) * 512)
                mm = nc.vector.scalar_tensor_tensor(scrapf[:], xt[:, sl],
                                                    1.0, xt[:, sl],
                                                    op0=ALU.mult, op1=ALU.mult,
                                                    accum_out=ssq[:, 4 * c + qq:4 * c + qq + 1])
            mm.then_inc(v_stat, 1)

        def ln_v_recip(vector, tg):
            c = tg % 2
            vector.wait_ge(s_u, tg + 1)
            nc.vector.reciprocal(rv[:, c:c + 1], u_c[:, c:c + 1]
                                 ).then_inc(v_rv, 1)

        def ln_v_evac(vector, tg, off):
            tl = tg - off
            vector.wait_ge(t_tr, tg + 1)
            if tg == NT1:
                vector.wait_ge(t_qkv, 48)
            nc.vector.tensor_copy(
                LNx.ap().rearrange("p (ch t) -> p ch t", ch=16)[:, :, tl * 128:(tl + 1) * 128],
                pstb.ap().rearrange("p (ch t) -> p ch t", ch=16),
            ).then_inc(v_ev, 1)

        def ln_s_unit(scalar, tl, off):
            tg = off + tl
            if tl >= 1:
                ln_s_part2(scalar, tg - 1)
            ln_s_part1(scalar, tg)

        def ln_s_part1(scalar, tg):
            c = tg % 2
            scalar.wait_ge(v_stat, tg + 1)
            nc.scalar.activation(m_c[:], ssum[:, c:c + 1], AF.Identity,
                                 scale=1.0 / C)
            nc.scalar.activation(msq[:], m_c[:], AF.Square)
            nc.scalar.activation(nmsq[:], msq[:], AF.Identity, scale=-1.0,
                                 bias=epsc[:, 0:1])
            nc.scalar.activation(sn1[:], ssq[:, 4 * c:4 * c + 1], AF.Identity,
                                 bias=ssq[:, 4 * c + 1:4 * c + 2])
            nc.scalar.activation(sn2[:], sn1[:], AF.Identity,
                                 bias=ssq[:, 4 * c + 2:4 * c + 3])
            nc.scalar.activation(sn3[:], sn2[:], AF.Identity,
                                 bias=ssq[:, 4 * c + 3:4 * c + 4])
            nc.scalar.drain()
            nc.scalar.activation(u_c[:, c:c + 1], sn3[:], AF.Identity,
                                 scale=1.0 / C, bias=nmsq[:, 0:1]
                                 ).then_inc(s_u, 1)

        def ln_s_part2(scalar, tg):
            c = tg % 2
            scalar.wait_ge(v_rv, tg + 1)
            nc.scalar.activation(r0c[:], rv[:, c:c + 1], AF.Sqrt)
            nc.scalar.activation(sn1[:], r0c[:], AF.Square)
            nc.scalar.drain()
            nc.scalar.activation(sn2[:], sn1[:], AF.Identity,
                                 scale=u_c[:, c:c + 1])
            nc.scalar.activation(sn3[:], sn2[:], AF.Identity, scale=-0.5,
                                 bias=c15[:, 0:1])
            nc.scalar.drain()
            nc.scalar.activation(r_col[:], sn3[:], AF.Identity,
                                 scale=r0c[:, 0:1])
            nc.scalar.activation(sn1[:], m_c[:], AF.Identity, scale=-1.0)
            nc.scalar.drain()
            nc.scalar.activation(negmr[:], sn1[:], AF.Identity,
                                 scale=r_col[:, 0:1])
            if tg >= 1:
                scalar.wait_ge(t_tr, tg)
            nc.scalar.drain()
            nc.scalar.activation(ln_tm[:], xt[:, c * C:(c + 1) * C], AF.Identity,
                                 bias=negmr[:, 0:1], scale=r_col[:, 0:1]
                                 ).then_inc(s_ln, 1)

        def wst_v_unit(vector, j):
            vector.wait_ge(ld_ws if j % 2 == 0 else ld_wsg,
                           16 * (j // 2 + 1))
            nc.vector.reduce_sum(absacc[:, j:j + 1],
                                 xt[:, (j % 2) * C:(j % 2 + 1) * C],
                                 axis=AX, apply_absolute_value=True
                                 ).then_inc(v_abs, 1)

        def ph1_vector(vector, p):
            off = NT1 * p
            for tl in range(NT1):
                ln_v_unit(vector, tl, off)
            ln_v_recip(vector, off + NT1 - 1)
            ln_v_evac(vector, off + NT1 - 2, off)
            ln_v_evac(vector, off + NT1 - 1, off)

        def ph1_scalar(scalar, p):
            off = NT1 * p
            for tl in range(NT1):
                ln_s_unit(scalar, tl, off)
            ln_s_part2(scalar, off + NT1 - 1)

        def ln_t_unit(tensor, tg):
            tensor.wait_ge(s_ln, tg + 1)
            if tg > 0:
                tensor.wait_ge(v_ev, tg)
            mm = None
            for i in range(16):
                mm = nc.tensor.transpose(pstb[:, i * 128:(i + 1) * 128],
                                         ln_tm[:, i * 128:(i + 1) * 128],
                                         ident[:])
            mm.then_inc(t_tr, 1)

        def ph1_tensor(tensor, p):
            if p == 1:
                tensor.wait_ge(v_vt, 16)
            for tl in range(NT1):
                ln_t_unit(tensor, NT1 * p + tl)

        def ph2_vector(vector, h):
            for g in range(8):
                vector.wait_ge(t_vtr, 32 * h + 4 * (g + 1))
                nc.vector.tensor_copy(
                    V_tm[:, g * 512:(g + 1) * 512],
                    pstb[:, (g % 2) * 1024:(g % 2) * 1024 + 512],
                ).then_inc(v_vt, 1)
            for t8 in range(8):
                gbase = CPH * h + ccum(t8)
                nkc = 4 * (t8 + 1)
                for kcl in range(4):
                    g = gbase + 4 * t8 + kcl
                    slot = g % 4
                    vector.wait_ge(s_exp, g + 1)
                    nc.vector.tensor_tensor(
                        P_sb[:, slot * 512:(slot + 1) * 512],
                        P_sb[:, slot * 512:(slot + 1) * 512],
                        masks[:, kcl * 512:(kcl + 1) * 512],
                        ALU.mult).then_inc(v_mk, 1)
                ti = 8 * h + t8
                vector.wait_ge(t_pv, gbase + nkc)
                if t8 == 0 and h > 0:
                    vector.wait_ge(st_y, 64 * h)
                nc.vector.tensor_copy(y_sb[:, t8 * 512:(t8 + 1) * 512], psY[:])
                if ti >= 2:
                    vector.wait_ge(q_l, 16 * (ti - 1))
                nc.vector.tensor_copy(
                    l_tile[0:1, (ti % 2) * 512:(ti % 2 + 1) * 512],
                    psL[0:1, :]).then_inc(v_fin, 1)
                if h == 1:
                    for tl in range(TA_LN[t8 - 1] if t8 else 0, TA_LN[t8]):
                        ln_v_unit(vector, tl, NT1)
                    if t8 == 7:
                        ln_v_recip(vector, 2 * NT1 - 1)
                        ln_v_evac(vector, 2 * NT1 - 2, NT1)
                        ln_v_evac(vector, 2 * NT1 - 1, NT1)
                if h >= 2:
                    for j in range(4 * t8, 4 * t8 + 4):
                        wst_v_unit(vector, 32 * (h - 2) + j)
                    if h == 3 and t8 == 7:
                        nc.vector.reduce_sum(stat1[:], absacc[:, 0:32], axis=AX)
                        nc.vector.reduce_sum(stat2[:], absacc[:, 32:64], axis=AX)
                        vector.drain()
                        vector.nop().then_inc(v_abs, 1)

        def ph2_scalar(scalar, h):
            b, hl = h // 2, h % 2
            if h > 0:
                scalar.wait_ge(st_y, 64 * h)
            for gi, (dst, bc) in enumerate(((K_fm, 4), (Q_fm, 0), (y_sb, 8))):
                for t8 in range(8):
                    grp = 24 * h + gi * 8 + t8
                    scalar.wait_ge(t_qkv, grp + 1)
                    nc.scalar.activation(
                        dst[:, t8 * 512:(t8 + 1) * 512],
                        ps_tr[:, (grp % 2) * 512:((grp % 2) + 1) * 512],
                        AF.Identity, bias=bcol[:, bc + h:bc + h + 1],
                        scale=1.0).then_inc(s_ev, 1)
            for t8 in range(8):
                gbase = CPH * h + ccum(t8)
                nkc = 4 * (t8 + 1)
                for kc in range(nkc):
                    g = gbase + kc
                    slot = g % 4
                    scalar.wait_ge(t_s, g + 1)
                    if g >= 4:
                        scalar.wait_ge(t_pv, g - 3)
                    nc.scalar.activation(
                        P_sb[:, slot * 512:(slot + 1) * 512],
                        ps_tr[:, 1024 + (g % 2) * 512:1024 + ((g % 2) + 1) * 512],
                        AF.Exp, scale=SCALE).then_inc(s_exp, 1)
                ti = 8 * h + t8
                scalar.wait_ge(v_fin, ti + 1)
                scalar.dma_start(
                    out=cc_l_in[(4 * b + t8 // 2) * 2 + hl:
                                (4 * b + t8 // 2) * 2 + hl + 1,
                                (t8 % 2) * 512:(t8 % 2) * 512 + 512],
                    in_=l_tile[0:1, (ti % 2) * 512:(ti % 2 + 1) * 512],
                ).then_inc(q_l, 16)
                if h == 1:
                    for tl in range(TA_LN[t8 - 1] if t8 else 0, TA_LN[t8]):
                        ln_s_unit(scalar, tl, NT1)
                    if t8 == 7:
                        ln_s_part2(scalar, 2 * NT1 - 1)
            scalar.wait_ge(v_fin, 8 * (h + 1))
            for qq in range(4):
                scalar.dma_start(
                    out=cc_y_in[(4 * b + qq) * 256 + hl * 128:
                                (4 * b + qq) * 256 + (hl + 1) * 128, :],
                    in_=y_sb[:, qq * TOK:(qq + 1) * TOK]).then_inc(st_y, 16)

        def ph2_tensor(tensor, h):
            tensor.wait_ge(ld_w, 48 * (h + 1))
            if h == 0:
                tensor.wait_ge(v_ev, NT1)
                tensor.wait_ge(v_on, 1)
            else:
                tensor.wait_ge(v_vt, 8 * h)
            if h == 2:
                tensor.wait_ge(v_ev, 2 * NT1)
            for gi, wsb in enumerate((wk_sb, wq_sb, wv_sb)):
                for t8 in range(8):
                    grp = 24 * h + gi * 8 + t8
                    if grp >= 2:
                        tensor.wait_ge(s_ev, grp - 1)
                    mm = None
                    for ch in range(16):
                        mm = nc.tensor.matmul(
                            ps_tr[:, (grp % 2) * 512:((grp % 2) + 1) * 512],
                            wsb[:, ch * 128:(ch + 1) * 128],
                            LNx[:, ch * T + t8 * 512: ch * T + (t8 + 1) * 512],
                            start=(ch == 0), stop=(ch == 15))
                    mm.then_inc(t_qkv, 1)
            tensor.wait_ge(s_ev, 24 * (h + 1))
            tensor.wait_ge(v_fin, 8 * h)
            for g in range(8):
                if 8 * h + g >= 2:
                    tensor.wait_ge(v_vt, 8 * h + g - 1)
                mm = None
                for i in range(4):
                    kc = 4 * g + i
                    mm = nc.tensor.transpose(
                        pstb[:, (g % 2) * 1024 + i * 128:(g % 2) * 1024 + (i + 1) * 128],
                        y_sb[:, kc * 128:(kc + 1) * 128],
                        ident[:])
                mm.then_inc(t_vtr, 4)
            tensor.wait_ge(v_vt, 8 * (h + 1))
            for t8 in range(8):
                gbase = CPH * h + ccum(t8)
                nkc = 4 * (t8 + 1)
                ti = 8 * h + t8
                for kc in range(nkc):
                    g = gbase + kc
                    if g >= 2:
                        tensor.wait_ge(s_exp, g - 1)
                    nc.tensor.matmul(
                        ps_tr[:, 1024 + (g % 2) * 512:1024 + ((g % 2) + 1) * 512],
                        K_fm[:, kc * 128:(kc + 1) * 128],
                        Q_fm[:, t8 * 512:(t8 + 1) * 512],
                        start=True, stop=True).then_inc(t_s, 1)
                    slot = g % 4
                    if kc >= 4 * t8:
                        tensor.wait_ge(v_mk, 32 * h + 4 * t8 + (kc - 4 * t8) + 1)
                    else:
                        tensor.wait_ge(s_exp, g + 1)
                    if kc == 0:
                        tensor.wait_ge(v_fin, ti)
                    nc.tensor.matmul(psL[:], ones_bf[:],
                                     P_sb[:, slot * 512:(slot + 1) * 512],
                                     start=(kc == 0), stop=(kc == nkc - 1))
                    nc.tensor.matmul(psY[:],
                                     V_tm[:, kc * 128:(kc + 1) * 128],
                                     P_sb[:, slot * 512:(slot + 1) * 512],
                                     start=(kc == 0), stop=(kc == nkc - 1)
                                     ).then_inc(t_pv, 1)
                if h == 1:
                    for tl in range(TT_LN[t8 - 1] if t8 else 0, TT_LN[t8]):
                        ln_t_unit(tensor, NT1 + tl)
                    if t8 == 7:
                        ln_t_unit(tensor, 2 * NT1 - 1)

        @block.vector
        def _(vector):
            nc.vector.memset(ones_bf[:], 1.0).then_inc(v_on, 1)
            nc.vector.memset(epsc[:], EPS)
            nc.vector.memset(c15[:], 1.5)
            vector.drain()
            ph1_vector(vector, 0)
            vector.wait_ge(ld_c, 48)
            ph2_vector(vector, 0)
            ph2_vector(vector, 1)
            ph2_vector(vector, 2)
            ph2_vector(vector, 3)

        @block.scalar
        def _(scalar):
            ph1_scalar(scalar, 0)
            scalar.wait_ge(ld_c, 48)
            ph2_scalar(scalar, 0)
            ph2_scalar(scalar, 1)
            ph2_scalar(scalar, 2)
            ph2_scalar(scalar, 3)

        @block.tensor
        def _(tensor):
            tensor.wait_ge(ld_c, 48)
            ph1_tensor(tensor, 0)
            ph2_tensor(tensor, 0)
            ph2_tensor(tensor, 1)
            ph2_tensor(tensor, 2)
            ph2_tensor(tensor, 3)

        es.close()


        ones_bf = oes.enter_context(nc.sbuf_tensor("ones_bf", [1, 128], BF16))
        ones_f1 = oes.enter_context(nc.sbuf_tensor("ones_f1", [1, 128], F32))
        ones_fc = oes.enter_context(nc.sbuf_tensor("ones_fc", [128, 1], F32))
        cmag = oes.enter_context(nc.sbuf_tensor("cmag", [128, 1], F32))
        negc = oes.enter_context(nc.sbuf_tensor("negc", [128, 1], F32))
        prA = oes.enter_context(nc.sbuf_tensor("prA", [128, 1], F32))
        prB = oes.enter_context(nc.sbuf_tensor("prB", [128, 1], F32))
        prF = oes.enter_context(nc.sbuf_tensor("prF", [128, 1], F32))
        bo_bf = oes.enter_context(nc.sbuf_tensor("bo_bf", [1, C], BF16))
        l_sb = oes.enter_context(nc.sbuf_tensor("l_sb", [128, 128], F32))
        rl128 = oes.enter_context(nc.sbuf_tensor("rl128", [128, 128], F32))
        s_cols = oes.enter_context(nc.sbuf_tensor("s_cols", [128, 4], F32))
        sc_small = oes.enter_context(nc.sbuf_tensor("sc_small", [1, 8], F32))
        au_all = oes.enter_context(nc.sbuf_tensor("au_all", [128, 8], F32))
        ad_all = oes.enter_context(nc.sbuf_tensor("ad_all", [128, 8], F32))
        pmax_all = oes.enter_context(nc.sbuf_tensor("pmax_all", [128, 64], F32))
        bn_out = oes.enter_context(nc.sbuf_tensor("bn_out", [128, 24], F32))
        mv = oes.enter_context(nc.sbuf_tensor("mv", [128, 2], F32))
        t_a = oes.enter_context(nc.sbuf_tensor("t_a", [128, 1], F32))
        t_b = oes.enter_context(nc.sbuf_tensor("t_b", [128, 1], F32))
        t_c = oes.enter_context(nc.sbuf_tensor("t_c", [128, 1], F32))
        r5 = oes.enter_context(nc.sbuf_tensor("r5", [128, 1], F32))
        s1c = oes.enter_context(nc.sbuf_tensor("s1c", [128, 1], F32))
        asc = oes.enter_context(nc.sbuf_tensor("asc", [128, 1], F32))
        abi = oes.enter_context(nc.sbuf_tensor("abi", [128, 1], F32))
        s2c = oes.enter_context(nc.sbuf_tensor("s2c", [128, 1], F32))
        psacc = oes.enter_context(nc.psum_tensor("psacc", [128, 2 * 512], F32))
        ps_tr = oes.enter_context(nc.psum_tensor("ps_tr", [128, 2048], BF16))
        ps_tr2 = oes.enter_context(nc.psum_tensor("ps_tr2", [128, 2048], BF16))
        psB = oes.enter_context(nc.psum_tensor("psB", [128, 512], F32))
        ld0 = oes.enter_context(nc.semaphore("ld0"))
        ldg = oes.enter_context(nc.semaphore("ldg"))
        ldw = oes.enter_context(nc.semaphore("ldw"))
        v_pr = oes.enter_context(nc.semaphore("v_pr"))
        t_st = oes.enter_context(nc.semaphore("t_st"))
        v_sfin = oes.enter_context(nc.semaphore("v_sfin"))
        v_rl = oes.enter_context(nc.semaphore("v_rl"))
        q_rl = oes.enter_context(nc.semaphore("q_rl"))
        ld_y = oes.enter_context(nc.semaphore("ld_y"))
        ld_xo = oes.enter_context(nc.semaphore("ld_xo"))
        t_bc = oes.enter_context(nc.semaphore("t_bc"))
        v_bc = oes.enter_context(nc.semaphore("v_bc"))
        v_yn = oes.enter_context(nc.semaphore("v_yn"))
        t_wo = oes.enter_context(nc.semaphore("t_wo"))
        v_x2 = oes.enter_context(nc.semaphore("v_x2"))
        v_x2b = oes.enter_context(nc.semaphore("v_x2b"))
        v_st5 = oes.enter_context(nc.semaphore("v_st5"))
        s_r5 = oes.enter_context(nc.semaphore("s_r5"))
        v_ab5 = oes.enter_context(nc.semaphore("v_ab5"))
        s_z5 = oes.enter_context(nc.semaphore("s_z5"))
        v_rq5 = oes.enter_context(nc.semaphore("v_rq5"))
        t_tr5 = oes.enter_context(nc.semaphore("t_tr5"))
        v_xq = oes.enter_context(nc.semaphore("v_xq"))
        s_q6 = oes.enter_context(nc.semaphore("s_q6"))
        v_q6 = oes.enter_context(nc.semaphore("v_q6"))
        t_tw6 = oes.enter_context(nc.semaphore("t_tw6"))
        v_tw6 = oes.enter_context(nc.semaphore("v_tw6"))
        t_up = oes.enter_context(nc.semaphore("t_up"))
        s_h6 = oes.enter_context(nc.semaphore("s_h6"))
        v_hm = oes.enter_context(nc.semaphore("v_hm"))
        v_st7 = oes.enter_context(nc.semaphore("v_st7"))
        v_rq7 = oes.enter_context(nc.semaphore("v_rq7"))
        t_tr7 = oes.enter_context(nc.semaphore("t_tr7"))
        v_hq = oes.enter_context(nc.semaphore("v_hq"))
        s_q8 = oes.enter_context(nc.semaphore("s_q8"))
        v_q8 = oes.enter_context(nc.semaphore("v_q8"))
        t_tw8 = oes.enter_context(nc.semaphore("t_tw8"))
        v_tw8 = oes.enter_context(nc.semaphore("v_tw8"))
        t_dn = oes.enter_context(nc.semaphore("t_dn"))
        s_o8 = oes.enter_context(nc.semaphore("s_o8"))
        v_o8 = oes.enter_context(nc.semaphore("v_o8"))
        st_out = oes.enter_context(nc.semaphore("st_out"))
        q_x2s = oes.enter_context(nc.semaphore("q_x2s"))
        q_x2r = oes.enter_context(nc.semaphore("q_x2r"))
        ldwg = oes.enter_context(nc.semaphore("ldwg"))
        with (
            nc.Block() as b0,
        ):
            @b0.sync
            def _(sync):
                sync.wait_ge(ccs, 2)
                sync.dma_start(
                    out=l_sb[:],
                    in_=cc_l_out.rearrange("g (r f) -> (g r) f", f=128),
                ).then_inc(ld0, 16)

            @b0.gpsimd
            def _(gpsimd):
                gpsimd.wait_ge(st_y, 16 * 4 * NHMY)
                gpsimd.wait_ge(q_l, 16 * 8 * NHMY)
                nc.gpsimd.collective_compute(
                    "AllToAll", ALU.bypass,
                    ins=[cc_y_in], outs=[cc_y_out],
                    replica_groups=[list(range(8))],
                ).then_inc(ccs, 1)
                nc.gpsimd.collective_compute(
                    "AllToAll", ALU.bypass,
                    ins=[cc_l_in], outs=[cc_l_out],
                    replica_groups=[list(range(8))],
                ).then_inc(ccs, 1)
                gpsimd.dma_start(out=bo_bf[:], in_=bo_d).then_inc(ldg, 16)

            @b0.scalar
            def _(scalar):
                scalar.wait_ge(v_rl, 1)
                scalar.dma_start(
                    out=rl_dram.rearrange("a (p f) -> (a p) f", p=128),
                    in_=rl128[:]).then_inc(q_rl, 16)

            @b0.vector
            def _(vector):
                nc.vector.memset(ones_bf[:], 1.0)
                nc.vector.memset(ones_f1[:], 1.0)
                nc.vector.memset(ones_fc[:], 1.0)
                nc.vector.memset(prA[:], 8388608.0)
                vector.drain()
                nc.vector.tensor_scalar_add(prB[:], prA[:], 0.25)
                vector.drain()
                nc.vector.tensor_sub(prF[:], prB[:], prA[:])
                vector.drain()
                nc.vector.tensor_scalar(cmag[:], prF[:], K54, C_MAGIC,
                                        op0=ALU.mult, op1=ALU.add)
                vector.drain()
                nc.vector.tensor_scalar_mul(negc[:], cmag[:], -1.0)
                vector.drain()
                vector.nop().then_inc(v_pr, 1)
                vector.nop().then_inc(v_pr, 1)
                vector.wait_ge(t_st, 2)
                nc.vector.tensor_scalar(sc_small[0:1, 0:2], psB[0:1, 0:2],
                                        1.0 / 8388608.0, EPS,
                                        op0=ALU.mult, op1=ALU.max)
                vector.drain()
                nc.vector.reciprocal(sc_small[0:1, 2:4], sc_small[0:1, 0:2])
                vector.drain()
                vector.nop().then_inc(v_pr, 1)
                vector.wait_ge(t_st, 3)
                nc.vector.tensor_copy(s_cols[:, 0:4], psB[:, 2:6]).then_inc(v_sfin, 1)
                vector.wait_ge(ld0, 16)
                nc.vector.reciprocal(rl128[:], l_sb[:]).then_inc(v_rl, 1)

            @b0.tensor
            def _(tensor):
                tensor.wait_ge(v_pr, 2)
                tensor.wait_ge(v_abs, 65)
                nc.tensor.matmul(psB[0:1, 0:1], ones_fc[:], stat1[:],
                                 start=True, stop=True)
                nc.tensor.matmul(psB[0:1, 1:2], ones_fc[:], stat2[:],
                                 start=True, stop=True).then_inc(t_st, 2)
                tensor.wait_ge(v_pr, 3)
                nc.tensor.matmul(psB[:, 2:3], ones_f1[:], sc_small[0:1, 2:3],
                                 start=True, stop=True)
                nc.tensor.matmul(psB[:, 3:4], ones_f1[:], sc_small[0:1, 0:1],
                                 start=True, stop=True)
                nc.tensor.matmul(psB[:, 4:5], ones_f1[:], sc_small[0:1, 3:4],
                                 start=True, stop=True)
                nc.tensor.matmul(psB[:, 5:6], ones_f1[:], sc_small[0:1, 1:2],
                                 start=True, stop=True).then_inc(t_st, 1)

        x2_es = contextlib.ExitStack()
        x2 = x2_es.enter_context(nc.sbuf_tensor("x2", [128, 8 * C], F32, side="right"))
        with (
            nc.sbuf_tensor("y_all", [128, 16 * TOK], BF16) as y_all,
            nc.sbuf_tensor("rl_buf", [1, 2 * TOK], F32) as rl_buf,
            nc.sbuf_tensor("rlb", [128, 512], F32) as rlb,
            nc.sbuf_tensor("wo_bf", [128, 2 * 16 * 512], BF16) as wo_bf,
            nc.sbuf_tensor("xo", [128, 2 * C], F32) as xo,
            nc.Block() as b4,
        ):
            @b4.sync
            def _(sync):

                sync.dma_start(
                    out=y_all.ap().rearrange("p (hc t) -> p hc t", hc=16),
                    in_=cc_y_out.rearrange("(hc p) t -> p hc t", p=128),
                ).then_inc(ld_y, 16)
                for tt in range(8):
                    if tt >= 2:
                        sync.wait_ge(v_x2b, tt - 1)
                    sync.dma_start(out=xo[:, (tt % 2) * C:(tt % 2 + 1) * C],
                                   in_=x_own[tt * 128:(tt + 1) * 128, :]
                                   ).then_inc(ld_xo, 16)

            @b4.scalar
            def _(scalar):
                scalar.wait_ge(q_rl, 16)
                for hc in range(16):
                    if hc >= 2:
                        scalar.wait_ge(t_bc, 2 * (hc - 1))
                    scalar.dma_start(
                        out=rl_buf[0:1, (hc % 2) * TOK:(hc % 2 + 1) * TOK],
                        in_=rl_dram[0:1, hc * TOK:(hc + 1) * TOK],
                    ).then_inc(q_rl, 16)

            @b4.gpsimd
            def _(gpsimd):
                for oc in range(4):
                    if oc >= 2:
                        gpsimd.wait_ge(t_wo, 8 * (oc - 1))
                    gpsimd.dma_start(
                        out=wo_bf.ap().rearrange("p (b hc o) -> p b hc o", b=2, hc=16)[:, oc % 2],
                        in_=Wo_d[:, oc * 512:(oc + 1) * 512].rearrange(
                            "(hc p) o -> p hc o", p=128),
                    ).then_inc(ldg, 16)

            @b4.tensor
            def _(tensor):
                tensor.wait_ge(v_sfin, 1)
                for hc in range(16):
                    for jj in range(2):
                        idx = 2 * hc + jj
                        tensor.wait_ge(q_rl, 16 * (hc + 2))
                        if idx > 0:
                            tensor.wait_ge(v_bc, idx)
                        nc.tensor.matmul(
                            psB[:, 0:512], ones_f1[:],
                            rl_buf[0:1, (hc % 2) * TOK + jj * 512:(hc % 2) * TOK + (jj + 1) * 512],
                            start=True, stop=True).then_inc(t_bc, 1)
                tensor.wait_ge(v_yn, 32)
                for oc in range(4):
                    tensor.wait_ge(ldg, 16 * (oc + 2))
                    for tt in range(8):
                        a = 8 * oc + tt
                        if a >= 2:
                            tensor.wait_ge(v_x2, a - 1)
                        for hc in range(16):
                            nc.tensor.matmul(
                                psacc[:, (a % 2) * 512:(a % 2 + 1) * 512],
                                y_all[:, hc * TOK + tt * 128:hc * TOK + (tt + 1) * 128],
                                wo_bf[:, (oc % 2) * 8192 + hc * 512:(oc % 2) * 8192 + (hc + 1) * 512],
                                start=(hc == 0), stop=False)
                        nc.tensor.matmul(
                            psacc[:, (a % 2) * 512:(a % 2 + 1) * 512],
                            ones_bf[:], bo_bf[0:1, oc * 512:(oc + 1) * 512],
                            start=False, stop=True).then_inc(t_wo, 1)

            @b4.vector
            def _(vector):
                vector.wait_ge(ld_y, 16)
                for hc in range(16):
                    for jj in range(2):
                        idx = 2 * hc + jj
                        vector.wait_ge(t_bc, idx + 1)
                        nc.vector.tensor_copy(rlb[:], psB[:, 0:512]).then_inc(v_bc, 1)
                        nc.vector.tensor_tensor(
                            y_all[:, hc * TOK + jj * 512:hc * TOK + (jj + 1) * 512],
                            y_all[:, hc * TOK + jj * 512:hc * TOK + (jj + 1) * 512],
                            rlb[:], ALU.mult).then_inc(v_yn, 1)
                for oc in range(4):
                    for tt in range(8):
                        a = 8 * oc + tt
                        vector.wait_ge(t_wo, a + 1)
                        nc.vector.tensor_copy(
                            x2[:, tt * C + oc * 512:tt * C + (oc + 1) * 512],
                            psacc[:, (a % 2) * 512:(a % 2 + 1) * 512],
                        ).then_inc(v_x2, 1)
                for tt in range(8):
                    vector.wait_ge(ld_xo, 16 * (tt + 1))
                    if tt == 0:
                        vector.wait_ge(v_x2, 32)
                    nc.vector.tensor_add(
                        x2[:, tt * C:(tt + 1) * C],
                        xo[:, (tt % 2) * C:(tt % 2 + 1) * C],
                        x2[:, tt * C:(tt + 1) * C]).then_inc(v_x2b, 1)

        xq_es = contextlib.ExitStack()
        xq_fm = xq_es.enter_context(nc.sbuf_tensor("xq_fm", [128, 16 * TOK], BF16))
        with (
            nc.sbuf_tensor("zq", [128, C], F32) as zq,
            nc.sbuf_tensor("xqt", [128, C], BF16) as xqt,
            nc.sbuf_tensor("xmax", [128, 1], F32) as xmax,
            nc.sbuf_tensor("xmin", [128, 1], F32) as xmin,
            nc.sbuf_tensor("s5sum", [128, 1], F32) as s5sum,
            nc.sbuf_tensor("s5sq", [128, 1], F32) as s5sq,
            nc.sbuf_tensor("u5", [128, 1], F32) as u5,
            nc.sbuf_tensor("m5", [128, 1], F32) as m5,
            nc.sbuf_tensor("r05", [128, 1], F32) as r05,
            nc.Block() as b5,
        ):
            @b5.vector
            def _(vector):
                vector.wait_ge(v_x2b, 8)
                vector.wait_ge(v_sfin, 1)
                for tt in range(8):
                    sl = slice(tt * C, (tt + 1) * C)
                    nc.vector.reduce_sum(s5sum[:], x2[:, sl], axis=AX)
                    nc.vector.scalar_tensor_tensor(zq[:], x2[:, sl], 1.0, x2[:, sl],
                                                   op0=ALU.mult, op1=ALU.mult,
                                                   accum_out=s5sq[:])
                    vector.drain()
                    nc.vector.tensor_scalar_mul(t_b[:], s5sum[:], 1.0 / C)
                    vector.drain()
                    nc.vector.scalar_tensor_tensor(t_c[:], t_b[:], -1.0, t_b[:],
                                                   op0=ALU.mult, op1=ALU.mult)
                    vector.drain()
                    nc.vector.scalar_tensor_tensor(t_a[:], s5sq[:], 1.0 / C, t_c[:],
                                                   op0=ALU.mult, op1=ALU.add)
                    vector.drain()
                    nc.vector.tensor_scalar_add(u5[:], t_a[:], EPS)
                    vector.drain()
                    nc.vector.tensor_copy(m5[:], t_b[:])
                    nc.vector.reciprocal(t_b[:], u5[:]).then_inc(v_st5, 1)
                    nc.vector.reduce_max(xmax[:], x2[:, sl], axis=AX)
                    nc.vector.tensor_reduce(xmin[:], x2[:, sl], axis=AX, op=ALU.min)
                    vector.drain()
                    vector.wait_ge(s_r5, tt + 1)
                    nc.vector.tensor_mul(t_a[:], r05[:], r05[:])
                    vector.drain()
                    nc.vector.tensor_mul(t_a[:], t_a[:], u5[:])
                    vector.drain()
                    nc.vector.tensor_scalar(t_a[:], t_a[:], -0.5, 1.5,
                                            op0=ALU.mult, op1=ALU.add)
                    vector.drain()
                    nc.vector.tensor_mul(r5[:], r05[:], t_a[:])
                    vector.drain()
                    nc.vector.tensor_sub(t_a[:], xmax[:], m5[:])
                    vector.drain()
                    nc.vector.tensor_sub(t_c[:], m5[:], xmin[:])
                    vector.drain()
                    nc.vector.tensor_max(t_a[:], t_a[:], t_c[:])
                    vector.drain()
                    nc.vector.tensor_mul(t_a[:], t_a[:], r5[:])
                    vector.drain()
                    nc.vector.tensor_scalar_max(t_a[:], t_a[:], EPS)
                    vector.drain()
                    nc.vector.reciprocal(t_c[:], t_a[:])
                    vector.drain()
                    nc.vector.tensor_scalar_mul(s1c[:], t_c[:], 127.0)
                    vector.drain()
                    nc.vector.tensor_mul(asc[:], r5[:], s1c[:])
                    vector.drain()
                    nc.vector.scalar_tensor_tensor(abi[:], m5[:], -1.0, asc[:],
                                                   op0=ALU.mult, op1=ALU.mult)
                    vector.drain()
                    nc.vector.scalar_tensor_tensor(au_all[:, tt:tt + 1], t_a[:],
                                                   1.0 / 127.0, s_cols[:, 1:2],
                                                   op0=ALU.mult, op1=ALU.mult)
                    vector.drain()
                    vector.nop().then_inc(v_ab5, 1)
                    vector.wait_ge(s_z5, tt + 1)
                    nc.vector.tensor_scalar(xqt[:], zq[:], cmag[:, 0:1], negc[:, 0:1],
                                            op0=ALU.add, op1=ALU.add).then_inc(v_rq5, 1)
                    vector.wait_ge(t_tr5, tt + 1)
                    nc.vector.tensor_copy(
                        xq_fm.ap().rearrange("p (hc t) -> p hc t", hc=16)[:, :, tt * 128:(tt + 1) * 128],
                        ps_tr.ap().rearrange("p (hc t) -> p hc t", hc=16),
                    ).then_inc(v_xq, 1)

            @b5.scalar
            def _(scalar):
                for tt in range(8):
                    scalar.wait_ge(v_st5, tt + 1)
                    nc.scalar.activation(r05[:], t_b[:], AF.Sqrt).then_inc(s_r5, 1)
                    scalar.wait_ge(v_ab5, tt + 1)
                    if tt >= 1:
                        scalar.wait_ge(v_rq5, tt)
                    nc.scalar.activation(zq[:], x2[:, tt * C:(tt + 1) * C],
                                         AF.Identity, bias=abi[:, 0:1],
                                         scale=asc[:, 0:1]).then_inc(s_z5, 1)

            @b5.tensor
            def _(tensor):
                for tt in range(8):
                    tensor.wait_ge(v_rq5, tt + 1)
                    if tt > 0:
                        tensor.wait_ge(v_xq, tt)
                    mm = None
                    for i in range(16):
                        mm = nc.tensor.transpose(ps_tr[:, i * 128:(i + 1) * 128],
                                                 xqt[:, i * 128:(i + 1) * 128],
                                                 ident[:])
                    mm.then_inc(t_tr5, 1)

            @b5.sync
            def _(sync):
                sync.wait_ge(s_z5, 8)
                sync.wait_ge(v_xq, 8)
                sync.dma_start(out=x2_dram, in_=x2[:]).then_inc(q_x2s, 16)
        x2_es.close()

        h_es = contextlib.ExitStack()
        h_sb = h_es.enter_context(nc.sbuf_tensor("h_sb", [128, 8 * I_FF], FP16, side="right"))
        with (
            nc.sbuf_tensor("wt6", [128, 2 * C], F32) as wt6,
            nc.sbuf_tensor("q1", [128, C], F32) as q1,
            nc.sbuf_tensor("q2", [128, C], F32) as q2,
            nc.sbuf_tensor("tnat", [128, C], BF16) as tnat,
            nc.sbuf_tensor("tup", [128, 16 * 512], BF16) as tup,
            nc.sbuf_tensor("pscr", [128, 512], F32) as pscr,
            nc.Block() as b6,
        ):
            @b6.sync
            def _(sync):
                sync.wait_ge(v_x2b, 8)
                sync.wait_ge(v_xq, 8)
                for j in range(0, 32, 2):
                    if j >= 2:
                        sync.wait_ge(v_q6, j - 1)
                    sync.dma_start(
                        out=wt6[:, 0:C],
                        in_=wup_d[j * 128:(j + 1) * 128, :],
                    ).then_inc(ldw, 16)

            @b6.gpsimd
            def _(gpsimd):
                gpsimd.wait_ge(v_x2b, 8)
                gpsimd.wait_ge(v_xq, 8)
                for j in range(1, 32, 2):
                    if j >= 2:
                        gpsimd.wait_ge(v_q6, j - 1)
                    gpsimd.dma_start(
                        out=wt6[:, C:2 * C],
                        in_=wup_d[j * 128:(j + 1) * 128, :],
                    ).then_inc(ldwg, 16)

            @b6.scalar
            def _(scalar):
                scalar.wait_ge(v_sfin, 1)
                scalar.wait_ge(v_pr, 1)
                scalar.wait_ge(q_x2s, 16)
                for Ig in range(8):
                    for i4 in range(4):
                        j = 4 * Ig + i4
                        scalar.wait_ge(ldw if j % 2 == 0 else ldwg,
                                       16 * (j // 2 + 1))
                        if j >= 1:
                            scalar.wait_ge(v_q6, j)
                        nc.scalar.activation(q1[:], wt6[:, (j % 2) * C:(j % 2 + 1) * C],
                                             AF.Identity, bias=cmag[:, 0:1],
                                             scale=s_cols[:, 0:1]).then_inc(s_q6, 1)
                    for tt in range(8):
                        g6 = 8 * Ig + tt
                        scalar.wait_ge(t_up, g6 + 1)
                        nc.scalar.activation(pscr[:], psacc[:, (g6 % 2) * 512:(g6 % 2 + 1) * 512],
                                             AF.Relu, scale=au_all[:, tt:tt + 1])
                        nc.scalar.activation(
                            h_sb[:, tt * I_FF + Ig * 512:tt * I_FF + (Ig + 1) * 512],
                            pscr[:], AF.Square).then_inc(s_h6, 1)

            @b6.vector
            def _(vector):
                for Ig in range(8):
                    for i4 in range(4):
                        j = 4 * Ig + i4
                        vector.wait_ge(s_q6, j + 1)
                        nc.vector.tensor_scalar(q2[:], q1[:], negc[:, 0:1], 1.0,
                                                op0=ALU.add, op1=ALU.min)
                        nc.vector.tensor_scalar(tnat[:], q2[:], -1.0, None,
                                                op0=ALU.max).then_inc(v_q6, 1)
                        vector.wait_ge(t_tw6, j + 1)
                        pst = ps_tr if j % 2 == 0 else ps_tr2
                        nc.vector.tensor_copy(
                            tup.ap().rearrange("p (hc i) -> p hc i", hc=16)[
                                :, :, i4 * 128:(i4 + 1) * 128],
                            pst.ap().rearrange("p (hc i) -> p hc i", hc=16),
                        ).then_inc(v_tw6, 1)
                    for tt in range(8):
                        g6 = 8 * Ig + tt
                        vector.wait_ge(s_h6, g6 + 1)
                        nc.vector.reduce_max(pmax_all[:, tt * 8 + Ig:tt * 8 + Ig + 1],
                                             h_sb[:, tt * I_FF + Ig * 512:tt * I_FF + (Ig + 1) * 512],
                                             axis=AX).then_inc(v_hm, 1)

            @b6.tensor
            def _(tensor):
                for Ig in range(8):
                    for i4 in range(4):
                        j = 4 * Ig + i4
                        tensor.wait_ge(v_q6, j + 1)
                        if j >= 2:
                            tensor.wait_ge(v_tw6, j - 1)
                        pst = ps_tr if j % 2 == 0 else ps_tr2
                        mm = None
                        for blk in range(16):
                            mm = nc.tensor.transpose(
                                pst[:, blk * 128:(blk + 1) * 128],
                                tnat[:, blk * 128:(blk + 1) * 128],
                                ident[:])
                        mm.then_inc(t_tw6, 1)
                    tensor.wait_ge(v_tw6, 4 * (Ig + 1))
                    for tt in range(8):
                        g6 = 8 * Ig + tt
                        if g6 >= 2:
                            tensor.wait_ge(s_h6, g6 - 1)
                        mm = None
                        for hc in range(16):
                            mm = nc.tensor.matmul(
                                psacc[:, (g6 % 2) * 512:(g6 % 2 + 1) * 512],
                                xq_fm[:, hc * TOK + tt * 128:hc * TOK + (tt + 1) * 128],
                                tup[:, hc * 512:(hc + 1) * 512],
                                start=(hc == 0), stop=(hc == 15))
                        mm.then_inc(t_up, 1)

        xq_es.close()
        hq_es = contextlib.ExitStack()
        hq_fm = hq_es.enter_context(nc.sbuf_tensor("hq_fm", [128, 32 * TOK], BF16))
        with (
            nc.sbuf_tensor("hq1", [128, 512], F32) as hq1,
            nc.sbuf_tensor("hqt", [128, 512], BF16) as hqt,
            nc.Block() as b7,
        ):
            @b7.vector
            def _(vector):
                vector.wait_ge(s_h6, 64)
                vector.wait_ge(v_hm, 64)
                for tt in range(8):
                    nc.vector.reduce_max(t_a[:], pmax_all[:, tt * 8:(tt + 1) * 8], axis=AX)
                    vector.drain()
                    nc.vector.tensor_scalar_max(t_a[:], t_a[:], EPS)
                    vector.drain()
                    nc.vector.reciprocal(t_b[:], t_a[:])
                    vector.drain()
                    nc.vector.tensor_scalar_mul(s2c[:], t_b[:], 127.0)
                    vector.drain()
                    nc.vector.scalar_tensor_tensor(ad_all[:, tt:tt + 1], t_a[:],
                                                   1.0 / 127.0, s_cols[:, 3:4],
                                                   op0=ALU.mult, op1=ALU.mult)
                    vector.drain()
                    vector.nop().then_inc(v_st7, 1)
                    for Ig in range(8):
                        idx = 8 * tt + Ig
                        nc.vector.tensor_scalar(
                            hq1[:], h_sb[:, tt * I_FF + Ig * 512:tt * I_FF + (Ig + 1) * 512],
                            s2c[:, 0:1], cmag[:, 0:1], op0=ALU.mult, op1=ALU.add)
                        nc.vector.tensor_scalar(hqt[:], hq1[:], negc[:, 0:1], None,
                                                op0=ALU.add).then_inc(v_rq7, 1)
                        vector.wait_ge(t_tr7, idx + 1)
                        pst = ps_tr if idx % 2 == 0 else ps_tr2
                        nc.vector.tensor_copy(
                            hq_fm.ap().rearrange("p (ic t) -> p ic t", ic=32)[
                                :, Ig * 4:(Ig + 1) * 4, tt * 128:(tt + 1) * 128],
                            pst.ap().rearrange("p (ic t) -> p ic t", ic=16)[:, 0:4],
                        ).then_inc(v_hq, 1)

            @b7.tensor
            def _(tensor):
                for tt in range(8):
                    for Ig in range(8):
                        idx = 8 * tt + Ig
                        tensor.wait_ge(v_rq7, idx + 1)
                        if idx >= 2:
                            tensor.wait_ge(v_hq, idx - 1)
                        pst = ps_tr if idx % 2 == 0 else ps_tr2
                        mm = None
                        for i in range(4):
                            mm = nc.tensor.transpose(
                                pst[:, i * 128:(i + 1) * 128],
                                hqt[:, i * 128:(i + 1) * 128],
                                ident[:])
                        mm.then_inc(t_tr7, 1)

        h_es.close()
        x2b_es = contextlib.ExitStack()
        x2 = x2b_es.enter_context(nc.sbuf_tensor("x2b", [128, 8 * C], F32, side="right"))
        with (
            nc.sbuf_tensor("wt8", [128, 2 * C], F32) as wt8,
            nc.sbuf_tensor("d1", [128, C], F32) as d1,
            nc.sbuf_tensor("d2", [128, C], F32) as d2,
            nc.sbuf_tensor("dnat", [128, C], BF16) as dnat,
            nc.sbuf_tensor("tdn", [128, 32 * 512], BF16) as tdn,
            nc.sbuf_tensor("pscr8", [128, 512], F32) as pscr8,
            nc.Block() as b8,
        ):
            @b8.sync
            def _(sync):
                sync.wait_ge(v_hq, 64)
                sync.dma_start(out=x2[:], in_=x2_dram).then_inc(q_x2r, 16)
                for j in range(0, 32, 2):
                    if j >= 2:
                        sync.wait_ge(s_q8, j - 1)
                    sync.dma_start(
                        out=wt8[:, 0:C],
                        in_=wdn_d[(j // 2) * 128:(j // 2 + 1) * 128, 0:2048],
                    ).then_inc(ldw, 16)
                sync.wait_ge(st_out, 16 * 4)

            @b8.gpsimd
            def _(gpsimd):
                gpsimd.wait_ge(v_hq, 64)
                for j in range(1, 32, 2):
                    if j >= 2:
                        gpsimd.wait_ge(s_q8, j - 1)
                    gpsimd.dma_start(
                        out=wt8[:, C:2 * C],
                        in_=wdn_d[(j // 2) * 128:(j // 2 + 1) * 128, 2048:4096],
                    ).then_inc(ldwg, 16)

            @b8.scalar
            def _(scalar):
                for Hg in range(4):
                    for h4 in range(4):
                        for half in range(2):
                            j = 8 * Hg + 2 * h4 + half
                            scalar.wait_ge(ldw if j % 2 == 0 else ldwg,
                                           16 * (16 + j // 2 + 1))
                            if j >= 1:
                                scalar.wait_ge(v_q8, j)
                            nc.scalar.activation(d1[:], wt8[:, (j % 2) * C:(j % 2 + 1) * C],
                                                 AF.Identity, bias=cmag[:, 0:1],
                                                 scale=s_cols[:, 2:3]).then_inc(s_q8, 1)
                    for tt in range(8):
                        g8 = 8 * Hg + tt
                        scalar.wait_ge(t_dn, g8 + 1)
                        if g8 > 0:
                            scalar.wait_ge(v_o8, g8)
                        nc.scalar.activation(pscr8[:], psacc[:, (g8 % 2) * 512:(g8 % 2 + 1) * 512],
                                             AF.Identity, scale=ad_all[:, tt:tt + 1]
                                             ).then_inc(s_o8, 1)
                    scalar.wait_ge(v_o8, 8 * (Hg + 1))
                    scalar.dma_start(
                        out=out_d.rearrange("(tt p) c -> p tt c", p=128)[
                            :, :, Hg * 512:(Hg + 1) * 512],
                        in_=x2.ap().rearrange("p (tt c) -> p tt c", tt=8)[
                            :, :, Hg * 512:(Hg + 1) * 512],
                    ).then_inc(st_out, 16)

            @b8.vector
            def _(vector):
                for Hg in range(4):
                    for h4 in range(4):
                        for half in range(2):
                            j = 8 * Hg + 2 * h4 + half
                            vector.wait_ge(s_q8, j + 1)
                            nc.vector.tensor_scalar(d2[:], d1[:], negc[:, 0:1], 1.0,
                                                    op0=ALU.add, op1=ALU.min)
                            nc.vector.tensor_scalar(dnat[:], d2[:], -1.0, None,
                                                    op0=ALU.max).then_inc(v_q8, 1)
                            vector.wait_ge(t_tw8, j + 1)
                            pst = ps_tr if j % 2 == 0 else ps_tr2
                            nc.vector.tensor_copy(
                                tdn.ap().rearrange("p (ic hh) -> p ic hh", ic=32)[
                                    :, half * 16:(half + 1) * 16,
                                    h4 * 128:(h4 + 1) * 128],
                                pst.ap().rearrange("p (ic hh) -> p ic hh", ic=16),
                            ).then_inc(v_tw8, 1)
                    for tt in range(8):
                        g8 = 8 * Hg + tt
                        vector.wait_ge(s_o8, g8 + 1)
                        if g8 == 0:
                            vector.wait_ge(q_x2r, 16)
                        nc.vector.tensor_add(
                            x2[:, tt * C + Hg * 512:tt * C + (Hg + 1) * 512],
                            pscr8[:],
                            x2[:, tt * C + Hg * 512:tt * C + (Hg + 1) * 512]
                        ).then_inc(v_o8, 1)

            @b8.tensor
            def _(tensor):
                for Hg in range(4):
                    if Hg > 0:
                        tensor.wait_ge(s_o8, 8 * Hg)
                    for h4 in range(4):
                        for half in range(2):
                            j = 8 * Hg + 2 * h4 + half
                            tensor.wait_ge(v_q8, j + 1)
                            if j >= 2:
                                tensor.wait_ge(v_tw8, j - 1)
                            pst = ps_tr if j % 2 == 0 else ps_tr2
                            mm = None
                            for blk in range(16):
                                mm = nc.tensor.transpose(
                                    pst[:, blk * 128:(blk + 1) * 128],
                                    dnat[:, blk * 128:(blk + 1) * 128],
                                    ident[:])
                            mm.then_inc(t_tw8, 1)
                    tensor.wait_ge(v_tw8, 8 * (Hg + 1))
                    for tt in range(8):
                        g8 = 8 * Hg + tt
                        if g8 >= 2:
                            tensor.wait_ge(s_o8, g8 - 1)
                        mm = None
                        for ic in range(32):
                            mm = nc.tensor.matmul(
                                psacc[:, (g8 % 2) * 512:(g8 % 2 + 1) * 512],
                                hq_fm[:, ic * TOK + tt * 128:ic * TOK + (tt + 1) * 128],
                                tdn[:, ic * 512:(ic + 1) * 512],
                                start=(ic == 0), stop=(ic == 31))
                        mm.then_inc(t_dn, 1)
        x2b_es.close()
        hq_es.close()

    return nc


_NC = None


def _get_program():
    global _NC
    if _NC is None:
        _NC = build_fused()
    return _NC


def _make_masks():
    p = np.arange(128)[:, None]
    f = np.arange(512)[None, :]
    m = np.zeros((128, 4 * 512), dtype=BF)
    for kcl in range(4):
        m[:, kcl * 512:(kcl + 1) * 512] = (f >= kcl * 128 + p).astype(BF)
    return m


def kernel(L0_attn_x, L0_attn_Wq, L0_attn_bq, L0_attn_Wk, L0_attn_bk,
           L0_attn_Wv, L0_attn_bv, L0_attn_Wo, L0_attn_bo,
           L0_ffn_w_up, L0_ffn_w_down):
    x = np.ascontiguousarray(np.asarray(L0_attn_x, dtype=np.float32))
    Wq = np.ascontiguousarray(np.asarray(L0_attn_Wq, dtype=np.float32))
    Wk = np.ascontiguousarray(np.asarray(L0_attn_Wk, dtype=np.float32))
    Wv = np.ascontiguousarray(np.asarray(L0_attn_Wv, dtype=np.float32))
    bq = np.asarray(L0_attn_bq, dtype=np.float32)
    bk = np.asarray(L0_attn_bk, dtype=np.float32)
    bv = np.asarray(L0_attn_bv, dtype=np.float32)
    Wo = np.ascontiguousarray(np.asarray(L0_attn_Wo, dtype=np.float32))
    bo = np.asarray(L0_attn_bo, dtype=np.float32).reshape(1, C)
    w_up = np.ascontiguousarray(np.asarray(L0_ffn_w_up, dtype=np.float32))
    w_down = np.ascontiguousarray(np.asarray(L0_ffn_w_down, dtype=np.float32))

    masks = _make_masks()
    ident = np.eye(128, dtype=BF)
    x_both = np.ascontiguousarray(x.reshape(2 * T, C))

    nc = _get_program()

    in_maps = []
    for c in range(8):
        b, q = c // 4, c % 4
        h0, h1 = 2 * c, 2 * c + 1
        wq_s = np.ascontiguousarray(np.stack([Wq[h0], Wq[h1], Wq[h0], Wq[h1]]))
        wk_s = np.ascontiguousarray(np.stack([Wk[h0], Wk[h1], Wk[h0], Wk[h1]]))
        wv_s = np.ascontiguousarray(np.stack([Wv[h0], Wv[h1], Wv[h0], Wv[h1]]))
        b_cols = np.zeros((128, 12), dtype=np.float32)
        for sl in range(4):
            hh = 2 * c + sl % 2
            b_cols[:, sl] = bq[hh]
            b_cols[:, 4 + sl] = bk[hh]
            b_cols[:, 8 + sl] = bv[hh]
        in_maps.append({
            "x_both": x_both,
            "wq": wq_s, "wk": wk_s, "wv": wv_s,
            "b_cols": b_cols,
            "masks": masks,
            "ident": ident,
            "x_own": np.ascontiguousarray(x[b, q * TOK:(q + 1) * TOK]),
            "Wo": Wo,
            "bo": bo,
            "w_up": w_up,
            "w_down": w_down,
        })
    res = run_bass_kernel_spmd(nc, in_maps, core_ids=list(range(8)))

    out = np.empty((2, T, C), dtype=np.float32)
    for c in range(8):
        b, q = c // 4, c % 4
        out[b, q * TOK:(q + 1) * TOK] = res.results[c]["out"]
    return out
